# version 02 (final)
# baseline (speedup 1.0000x reference)
import math
import numpy as np
import ml_dtypes
import concourse.bass as bass
import concourse.mybir as mybir
from contextlib import ExitStack
from concourse.bass_utils import run_bass_kernel_spmd

ENGS = ("sync", "scalar", "vector", "tensor", "gpsimd")


class Step:
    __slots__ = ("eng", "emit", "deps", "dma", "inc", "cum", "idx", "w", "kind")

    def __init__(self, eng, emit, deps, dma, w=None, kind=None):
        self.eng, self.emit, self.deps, self.dma = eng, emit, list(deps), dma
        self.inc = dma or (w is not None)
        self.cum = None
        self.idx = None
        self.w = w
        self.kind = kind


class Prog:
    def __init__(self, name):
        self.name = name
        self.steps = {e: [] for e in ENGS}
        self.order = []

    def add(self, eng, emit, deps=(), dma=False, w=None):
        s = Step(eng, emit, [d for d in deps if d is not None], dma, w)
        s.idx = len(self.steps[eng])
        self.steps[eng].append(s)
        self.order.append(s)
        return s

    def sync(self, emit, deps=(), dma=True):
        return self.add("sync", emit, deps, dma)

    def scalar(self, emit, deps=(), dma=False):
        return self.add("scalar", emit, deps, dma)

    def vector(self, emit, deps=()):
        return self.add("vector", emit, deps, False)

    def tensor(self, emit, deps=()):
        return self.add("tensor", emit, deps, False)

    def gpsimd(self, emit, deps=(), dma=False):
        return self.add("gpsimd", emit, deps, dma)

    def selfwait(self, eng):
        s = Step(eng, None, [], False, kind="selfwait")
        s.idx = len(self.steps[eng])
        self.steps[eng].append(s)
        self.order.append(s)
        return s

    def barrier(self):
        lasts = [[s for s in self.steps[e] if s.kind != "selfwait"][-1]
                 for e in ENGS if any(s.kind != "selfwait" for s in self.steps[e])]
        outs = []
        for e in ENGS:
            outs.append(self.add(e, lambda eng: eng.nop(),
                                 [s for s in lasts if s.eng != e]))
        return outs

    def compile(self, nc, es, final_block=False):
        for s in self.order:
            for d in s.deps:
                if d.eng != s.eng:
                    d.inc = True
        for e in ENGS:
            c = 0
            for s in self.steps[e]:
                if s.inc:
                    c += s.w if s.w is not None else (16 if s.dma else 1)
                s.cum = c
        sems = {e: es.enter_context(nc.semaphore(f"{self.name}_{e}"))
                for e in ENGS if self.steps[e]}

        def make(ename):
            def body(eng):
                for s in self.steps[ename]:
                    waits = {}
                    for d in s.deps:
                        if d.eng != ename:
                            waits[d.eng] = max(waits.get(d.eng, 0), d.cum)
                        else:
                            assert d.idx < s.idx, f"same-engine dep must precede ({self.name})"
                    for e2, v in waits.items():
                        eng.wait_ge(sems[e2], v)
                    if s.kind == "selfwait":
                        if s.cum:
                            eng.wait_ge(sems[ename], s.cum)
                        continue
                    if s.w is not None:
                        s.emit(eng, sems[ename])
                        continue
                    instr = s.emit(eng)
                    if s.inc:
                        if instr is None:
                            instr = eng.nop()
                        instr.then_inc(sems[ename], 16 if s.dma else 1)
                    if s.dma:
                        eng.wait_ge(sems[ename], s.cum)
            return body

        with nc.Block(self.name) as blk:
            for ename in ENGS:
                if not self.steps[ename]:
                    continue
                getattr(blk, ename)(make(ename))
        return sems


import numpy as np
import ml_dtypes
import concourse.bass as bass
import concourse.mybir as mybir
from contextlib import ExitStack

F32 = mybir.dt.float32
BF16 = mybir.dt.bfloat16
I32 = mybir.dt.int32
AF = mybir.ActivationFunctionType
ALU = mybir.AluOpType
AX = mybir.AxisListType.X
ET = mybir.EngineType

B, SEQ, D = 2, 4096, 2048
H, HG, DH, RD, RK = 16, 4, 128, 64, 512
DFF, HID = 8192, 5632
OWN, PERS, HALO = 1024, 16, 128
TOK = PERS + HALO + OWN
NE = D // 128
NH1 = DFF // 128
NH2 = HID // 128
NCH = SEQ // 512
SC0 = 1.0 / math.sqrt(DH + RD)
SC1 = 1.0 / math.sqrt(DH)
EPS = 1e-5
BF = ml_dtypes.bfloat16


def stage_inputs(inp):
    C = np.ascontiguousarray
    x = np.asarray(inp["L0_attn_x"])
    cosT = C(np.asarray(inp["L0_attn_cos_table"]).T.astype(np.float32))
    sinT = C(np.asarray(inp["L0_attn_sin_table"]).T.astype(np.float32))
    ropeC = C(np.concatenate([cosT, cosT], axis=0))
    ropeS = C(np.concatenate([sinT, sinT], axis=0))
    wq = np.asarray(inp["L0_attn_wq"])
    wkv = np.asarray(inp["L0_attn_wkv_down"])
    wup = np.asarray(inp["L0_attn_w_up"])
    wout = np.asarray(inp["L0_attn_w_out"])
    w1 = np.asarray(inp["L0_ffn_w1"]); b1 = np.asarray(inp["L0_ffn_b1"])
    w2 = np.asarray(inp["L0_ffn_w2"]); b2 = np.asarray(inp["L0_ffn_b2"])
    l1wq = np.asarray(inp["L1_attn_Wq"]); l1wkv = np.asarray(inp["L1_attn_Wkv"])
    l1wk = np.asarray(inp["L1_attn_Wk"]); l1wv = np.asarray(inp["L1_attn_Wv"])
    l1wo = np.asarray(inp["L1_attn_Wo"])
    g1 = np.asarray(inp["L1_ffn_wq_gate"]); sg = np.asarray(inp["L1_ffn_s_gate"])
    u1 = np.asarray(inp["L1_ffn_wq_up"]); su = np.asarray(inp["L1_ffn_s_up"])
    d1 = np.asarray(inp["L1_ffn_wq_down"]); sd = np.asarray(inp["L1_ffn_s_down"])

    def reorder_qcols(h):
        base = 192 * h
        cols = list(range(base, base + DH)) + \
               list(range(base + DH, base + 192, 2)) + \
               list(range(base + DH + 1, base + 192, 2))
        return wq[:, cols]

    wkv_r = C(np.concatenate([wkv[:, :RK], wkv[:, RK::2], wkv[:, RK + 1::2]],
                             axis=1).astype(np.float32))
    w1s = C(w1.T.reshape(D, NH1, 128).transpose(1, 0, 2).astype(np.float32))
    w2s = C(w2.T.reshape(DFF, NE, 128).transpose(1, 0, 2).astype(np.float32))
    b1p = C(b1.reshape(NH1, 128).T.astype(np.float32))
    b2p = C(b2.reshape(NE, 128).T.astype(np.float32))
    g1s = C(g1.T.reshape(D, NH2, 128).transpose(1, 0, 2).astype(BF))
    u1s = C(u1.T.reshape(D, NH2, 128).transpose(1, 0, 2).astype(BF))
    d1s = C(d1.T.reshape(HID, NE, 128).transpose(1, 0, 2).astype(BF))
    sgp = C(sg.reshape(NH2, 128).T.astype(np.float32))
    sup = C(su.reshape(NH2, 128).T.astype(np.float32))
    sdp = C(sd.reshape(NE, 128).T.astype(np.float32))
    l1wos = C(l1wo.reshape(D, NE, 128).transpose(1, 0, 2).astype(np.float32))

    pcol = np.arange(128)[:, None]
    fcol = np.arange(512)[None, :]
    l0m = C(np.stack([(128 * m + pcol <= fcol) for m in range(4)],
                     axis=1).astype(BF))
    swp = np.zeros((64, 64), dtype=BF)
    for k in range(64):
        swp[k, (k + 32) % 64] = 1.0

    perm = np.empty(3072, dtype=np.int64)
    for h in range(H):
        perm[768 * (h // 4) + 192 * (h % 4) + np.arange(192)] = 192 * h + np.arange(192)
    wout_p = wout[perm]
    wouts = C(np.stack([wout_p[:, 128 * e:128 * (e + 1)] for e in range(NE)])
              .astype(BF))

    in_maps = []
    for c in range(8):
        b, s = divmod(c, 4)
        S0 = OWN * s
        xb = x[b]
        cols = np.concatenate([np.arange(PERS),
                               np.arange(max(0, S0 - HALO), max(0, S0 - HALO) + HALO),
                               np.arange(S0, S0 + OWN)])
        xres = C(xb[cols].T.astype(np.float32))
        if s == 0:
            xres[:, PERS:PERS + HALO] = 0.0
        wq_c = C(np.concatenate([reorder_qcols(4 * s + j) for j in range(HG)],
                                axis=1).astype(np.float32))
        wup_c = C(np.concatenate(
            [wup[:, 320 * (4 * s + j):320 * (4 * s + j + 1)] for j in range(HG)],
            axis=1).astype(np.float32))

        qg = S0 + 128 * np.arange(8)[:, None, None] + np.arange(128)[None, None, :]
        pp = np.arange(128)[None, :, None]
        kgp = S0 + 128 * np.arange(8)[:, None, None] - 128 + pp
        m_prev = ((kgp >= 0) & (kgp <= qg) & (qg - kgp < HALO)).astype(BF)
        kgd = S0 + 128 * np.arange(8)[:, None, None] + pp
        m_diag = ((kgd <= qg) & (qg - kgd < HALO)).astype(BF)
        pp16 = np.arange(16)[None, :, None]
        m_pers = ((pp16 <= qg) & (qg - pp16 >= HALO)).astype(BF)

        mu = np.zeros((128, 4), dtype=np.float32)
        mu[:, s] = 1.0
        in_maps.append(dict(
            mub=mu,
            xfullT=C(xb.T.astype(np.float32)), xresT=xres,
            ropeC=ropeC, ropeS=ropeS, swp=C(swp),
            wq=wq_c, wkv=wkv_r, wup=wup_c, wouts=wouts,
            w1s=w1s, w2s=w2s, b1p=b1p, b2p=b2p,
            l1wqs=C(l1wq.reshape(D, NE, 128).transpose(1, 0, 2).astype(np.float32)),
            l1wkvs=C(l1wkv.reshape(D, 4, 128).transpose(1, 0, 2).astype(np.float32)),
            l1wk=C(l1wk.astype(np.float32)), l1wv=C(l1wv.astype(np.float32)),
            l1wos=l1wos, g1s=g1s, u1s=u1s, d1s=d1s, sgp=sgp, sup=sup, sdp=sdp,
            l0m=l0m,
            l1mA=C(np.stack([m_prev, m_diag], axis=1).transpose(2, 0, 1, 3)
                   .reshape(128, 2048)),
            l1mP=C(m_pers.transpose(1, 0, 2).reshape(16, 1024)),
        ))
    return in_maps


def build(upto=8, dbg=()):
    nc = bass.Bass("TRN2", target_bir_lowering=False, debug=False)
    I = lambda n, s, d=F32: nc.dram_tensor(n, s, d, kind="ExternalInput").ap()
    O = lambda n, s, d=F32: nc.dram_tensor(n, s, d, kind="ExternalOutput").ap()
    DS = lambda n, s, d=F32: nc.dram_tensor(n, s, d).ap()

    xfullT = I("xfullT", [D, SEQ])
    xresT = I("xresT", [D, TOK])
    mub = I("mub", [128, 4])
    ropeC = I("ropeC", [64, SEQ]); ropeS = I("ropeS", [64, SEQ])
    swp = I("swp", [64, 64], BF16)
    wq = I("wq", [D, HG * 192]); wkv = I("wkv", [D, 576])
    wup = I("wup", [RK, HG * 320])
    wouts = I("wouts", [NE, 3072, 128], BF16)
    w1s = I("w1s", [NH1, D, 128]); w2s = I("w2s", [NE, DFF, 128])
    b1p = I("b1p", [128, NH1]); b2p = I("b2p", [128, NE])
    l1wqs = I("l1wqs", [NE, D, 128]); l1wkvs = I("l1wkvs", [4, D, 128])
    l1wk = I("l1wk", [RK, D]); l1wv = I("l1wv", [RK, D])
    l1wos = I("l1wos", [NE, D, 128])
    g1si = I("g1s", [NH2, D, 128], BF16); u1si = I("u1s", [NH2, D, 128], BF16)
    d1si = I("d1s", [NE, HID, 128], BF16)
    sgp = I("sgp", [128, NH2]); sup = I("sup", [128, NH2]); sdp = I("sdp", [128, NE])
    l0m = I("l0m", [128, 4, 512], BF16)
    l1mA = I("l1mA", [128, 2048], BF16)
    l1mP = I("l1mP", [16, 1024], BF16)
    y = O("y", [D, OWN])

    avd = DS("avd", [HG * 192, SEQ], BF16)
    ldm = DS("ldm", [HG, SEQ])
    avg = DS("avg", [H * 192, SEQ], BF16)
    lg = DS("lg", [H, SEQ])
    x1d = DS("x1d", [D, TOK])
    x2d = DS("x2d", [D, TOK])
    x3d = DS("x3d", [D, OWN])

    dbg_out = {}
    def DO(name, shape, dt=F32):
        if name in dbg:
            dbg_out[name] = O("dbg_" + name, shape, dt)
        return dbg_out.get(name)

    d_xln = DO("xln", [D, SEQ], BF16)
    d_q = DO("q", [HG * 192, SEQ], BF16)
    d_ckv = DO("ckv", [RK + 128, SEQ], BF16)
    d_av = DO("av", [HG * 192, SEQ], BF16)
    d_l = DO("l", [HG, SEQ])
    d_nrm = DO("nrm", [2, SEQ])
    d_cdbg = DO("cdbg", [HG, 8])
    d_x1 = DO("x1", [D, TOK])
    d_x2 = DO("x2", [D, TOK])
    d_x3 = DO("x3", [D, OWN])
    d_cthr = DO("cthr", [128, 4])
    d_cstat = DO("cstat", [2, TOK // 2])
    d_avgf = DO("avgf", [H * 192, SEQ], BF16)
    d_lgf = DO("lgf", [H, SEQ])
    d_avgs = DO("avgs", [128, 24 * TOK], BF16)
    d_avgp = DO("avgp", [128, 24 * TOK], BF16)
    d_lsbf = DO("lsbf", [16, TOK])
    d_aof = DO("aof", [128, NE * TOK])
    d_cxln = DO("cxln", [128, NE * (TOK // 2)], BF16)
    d_chid = DO("chid", [128, NH1 * (TOK // 2)], BF16)

    ges = ExitStack()
    with ges:
        SB = lambda n, s, d=F32: ges.enter_context(nc.sbuf_tensor(n, s, d))
        PSUM = lambda n, s: ges.enter_context(nc.psum_tensor(n, s, F32))

        ssy = ges.enter_context(nc.semaphore("ssy"))
        ssyc = [0]

        def edge(eng, instr):
            ssyc[0] += 1
            instr.then_inc(ssy, 1)
            eng.wait_ge(ssy, ssyc[0])
            return instr

        ones128f = SB("ones128f", [128, 1])
        ones1x = SB("ones1x", [1, 128])
        ones16f = SB("ones16f", [16, 1])
        ones128b = SB("ones128b", [128, 1], BF16)
        ones16b = SB("ones16b", [16, 1], BF16)
        swpb = SB("swpb", [64, 64], BF16)

        pr0 = Prog("ini")
        pr0.vector(lambda v: (v.memset(ones128f[:], 1.0),
                              v.memset(ones1x[:], 1.0),
                              v.memset(ones16f[:], 1.0),
                              v.memset(ones128b[:], 1.0),
                              v.memset(ones16b[:], 1.0))[-1])
        pr0.sync(lambda sy: sy.dma_start(out=swpb[:], in_=swp[:, :]))
        pr0.barrier()
        pr0.compile(nc, ges)

        esA = ExitStack()
        with esA:
            A = lambda n, s, d=F32: esA.enter_context(nc.sbuf_tensor(n, s, d))
            AP_ = lambda n, s: esA.enter_context(nc.psum_tensor(n, s, F32))
            wupb = A("wupb", [128, 4 * HG * 320], BF16)
            ckv = [A(f"ckv{r}", [128, SEQ], BF16) for r in range(4)]
            krope2 = A("krope2", [128, SEQ], BF16)
            qa = [A(f"qa{j}", [128, SEQ], BF16) for j in range(HG)]
            qbp = [A(f"qbp{j}", [128, SEQ], BF16) for j in range(2)]
            rCb = A("rCb", [64, SEQ], BF16); rSb = A("rSb", [64, SEQ], BF16)
            l0mb = A("l0mb", [128, 4 * 512], BF16)
            pstA = AP_("pstA", [1, 512]); pstB = AP_("pstB", [1, 512])
            pS = [AP_("pS0", [128, 512]), AP_("pS1", [128, 512])]
            pava = AP_("pava", [128, 512]); pavb = AP_("pavb", [65, 512])
            pswp = AP_("pswp", [64, 512])
            pnn = AP_("pnn", [1, 512])

            es1 = ExitStack()
            A1 = lambda n, s, d=F32: es1.enter_context(nc.sbuf_tensor(n, s, d))
            wqb = A1("wqb", [128, NE * HG * 192], BF16)
            wkvb = A1("wkvb", [128, NE * 576], BF16)
            wstg = [A1("wstg0", [128, 1280])] * 2
            xin = [A1(f"xin{i}", [128, 512]) for i in range(2)]
            xs2 = [A1(f"xs2{i}", [128, 512]) for i in range(2)]
            xlnb = A1("xlnb", [128, NE * 512], BF16)
            sq = [A1(f"sq{i}", [128, 512]) for i in range(2)]
            stv = A1("stv", [1, 6 * 512])
            rsc = A1("rsc", [64, 512], BF16)

            pr = Prog("pA")
            tabs = pr.gpsimd(lambda g: g.dma_start(out=rCb[:], in_=ropeC[:, :]), dma=True)
            tabs2 = pr.gpsimd(lambda g: g.dma_start(out=rSb[:], in_=ropeS[:, :]), dma=True)
            mk0 = pr.sync(lambda sy: sy.dma_start(
                out=l0mb[:], in_=l0m.rearrange("p m f -> p (m f)")))
            wtiles = [("wq", c) for c in range(NE)] + \
                     [("wkv", c) for c in range(NE)] + \
                     [("wup", r) for r in range(4)]
            casts = []
            for i, (kind, c) in enumerate(wtiles):
                src = dict(wq=wq, wkv=wkv, wup=wup)[kind]
                wid = dict(wq=HG * 192, wkv=576, wup=HG * 320)[kind]
                lddep = [casts[i - 1]] if i >= 1 else []
                ldst = pr.sync(lambda sy, i=i, src=src, wid=wid, c=c:
                               sy.dma_start(out=wstg[i % 2][:, 0:wid],
                                            in_=src[128 * c:128 * (c + 1), :]),
                               deps=lddep)
                def cast(v, kind=kind, c=c, i=i):
                    if kind == "wq":
                        return v.tensor_scalar(wqb[:, 768 * c:768 * (c + 1)],
                                               wstg[i % 2][:, 0:768], SC0, None, ALU.mult)
                    if kind == "wkv":
                        return v.tensor_copy(wkvb[:, 576 * c:576 * (c + 1)],
                                             wstg[i % 2][:, 0:576])
                    return v.tensor_copy(wupb[:, 1280 * c:1280 * (c + 1)],
                                         wstg[i % 2][:, 0:1280])
                casts.append(pr.vector(cast, deps=[ldst]))

            prev = dict(p1last=None, vst=None, xln_last=None, rb=None,
                        ropeA=[None], ropeB=[None])
            qev, ckev = [[None] * NCH for _ in range(HG)], [[None] * NCH for _ in range(4)]
            krdups = []
            for t in range(NCH):
                lds, mmAs, mmBs, sqs = [], [], [], []
                for c in range(NE):
                    deps = []
                    if len(mmAs) >= 2:
                        deps += [mmAs[c - 2], sqs[c - 2]]
                    elif prev["p1last"] is not None:
                        deps.append(prev["p1last"][c - 2])
                    lds.append(pr.sync(lambda sy, c=c, t=t: sy.dma_start(
                        out=xin[c % 2][:],
                        in_=xfullT[128 * c:128 * (c + 1), 512 * t:512 * (t + 1)]),
                        deps=deps))
                    d0 = [lds[c]] + ([prev["vst"]] if c == 0 and prev["vst"] else [])
                    mmAs.append(pr.tensor(lambda te, c=c: te.matmul(
                        pstA[:], ones128f[:], xin[c % 2][:],
                        start=(c == 0), stop=(c == NE - 1), skip_group_check=True),
                        deps=d0))
                    sqdep = [lds[c]]
                    if c >= 2:
                        sqdep.append(mmBs[c - 2])
                    sqs.append(pr.scalar(lambda s, c=c: s.activation(
                        out=sq[c % 2][:], in_=xin[c % 2][:],
                        func=AF.Square), deps=sqdep))
                    d1_ = [sqs[c]] + ([prev["vst"]] if c == 0 and prev["vst"] else [])
                    mmBs.append(pr.tensor(lambda te, c=c: te.matmul(
                        pstB[:], ones128f[:], sq[c % 2][:],
                        start=(c == 0), stop=(c == NE - 1), skip_group_check=True),
                        deps=d1_))

                vst = pr.vector(lambda v: (
                    v.tensor_scalar(stv[:, 0:512], pstA[:], 1.0 / D, None, ALU.mult),
                    v.tensor_scalar(stv[:, 1024:1536], pstB[:], 1.0 / D, EPS,
                                    ALU.mult, ALU.add),
                    v.tensor_tensor(stv[:, 512:1024], stv[:, 0:512], stv[:, 0:512],
                                    ALU.mult),
                    v.tensor_tensor(stv[:, 1024:1536], stv[:, 1024:1536],
                                    stv[:, 512:1024], ALU.subtract))[-1],
                    deps=[mmAs[-1], mmBs[-1]])
                ssq = pr.scalar(lambda s: s.activation(
                    out=stv[:, 1536:2048], in_=stv[:, 1024:1536], func=AF.Sqrt),
                    deps=[vst])
                vnt = pr.vector(lambda v: (
                    edge(v, v.reciprocal(stv[:, 512:1024], stv[:, 1536:2048])),
                    v.tensor_tensor(stv[:, 1536:2048], stv[:, 1024:1536],
                                    stv[:, 512:1024], ALU.mult),
                    v.tensor_tensor(stv[:, 1536:2048], stv[:, 1536:2048],
                                    stv[:, 512:1024], ALU.mult),
                    v.tensor_scalar(stv[:, 1536:2048], stv[:, 1536:2048], -0.5,
                                    1.5, ALU.mult, ALU.add),
                    v.tensor_tensor(stv[:, 2048:2560], stv[:, 512:1024],
                                    stv[:, 1536:2048], ALU.mult),
                    v.tensor_tensor(stv[:, 2560:3072], stv[:, 0:512],
                                    stv[:, 2048:2560], ALU.mult))[-1],
                    deps=[ssq])
                rbdep = [vnt] + ([prev["xln_last"]] if prev["xln_last"] else [])
                rb = pr.tensor(lambda te: (
                    te.matmul(pS[0][:], ones1x[:], stv[:, 2048:2560],
                              start=True, stop=True),
                    te.matmul(pS[1][:], ones1x[:], stv[:, 2560:3072],
                              start=True, stop=True))[-1], deps=rbdep)
                xlns, lds2 = [], []
                for c in range(NE):
                    if c < 2:
                        ldd = [mmAs[NE - 2 + c], sqs[NE - 2 + c]]
                    else:
                        ldd = [xlns[c - 2]]
                    lds2.append(pr.sync(lambda sy, c=c, t=t: sy.dma_start(
                        out=xin[c % 2][:],
                        in_=xfullT[128 * c:128 * (c + 1), 512 * t:512 * (t + 1)]),
                        deps=ldd))
                    dd = [rb, lds2[c]]
                    if c >= 2:
                        dd.append(xlns[c - 2])
                    xd = pr.vector(lambda v, c=c: (
                        v.tensor_tensor(xs2[c % 2][:], xin[c % 2][:],
                                        pS[0][:], ALU.mult),
                        v.tensor_tensor(xlnb[:, 512 * c:512 * (c + 1)], xs2[c % 2][:],
                                        pS[1][:], ALU.subtract))[-1], deps=dd)
                    xlns.append(xd)
                if d_xln is not None:
                    pr.gpsimd(lambda g, t=t: g.dma_start(
                        out=d_xln[:, 512 * t:512 * (t + 1)],
                        in_=xlnb.ap().rearrange("p (c f) -> (c p) f", f=512) if False
                        else xlnb[:, 0:512]), deps=[xlns[-1]], dma=True) if False else None
                grp_ev = []
                for gidx in range(13):
                    dst = pS[gidx % 2]
                    dd = [xlns[-1]]
                    if gidx >= 2:
                        dd.append(grp_ev[gidx - 2])

                    def gmm(te, gidx=gidx, dst=dst, t=t):
                        last = None
                        for c in range(NE):
                            if gidx < 4:
                                lhs = wqb[:, 768 * c + 192 * gidx:768 * c + 192 * gidx + 128]
                                o = dst[:]
                            elif gidx < 8:
                                r = gidx - 4
                                lhs = wkvb[:, 576 * c + 128 * r:576 * c + 128 * (r + 1)]
                                o = dst[:]
                            elif gidx < 12:
                                j = gidx - 8
                                lhs = wqb[:, 768 * c + 192 * j + 128:768 * c + 192 * j + 192]
                                o = dst[0:64, :]
                            else:
                                lhs = wkvb[:, 576 * c + 512:576 * c + 576]
                                o = dst[0:64, :]
                            last = te.matmul(o, lhs, xlnb[:, 512 * c:512 * (c + 1)],
                                             start=(c == 0), stop=(c == NE - 1))
                        return last
                    gm = pr.tensor(gmm, deps=dd)
                    if gidx < 4:
                        ev = pr.scalar(lambda s, gidx=gidx, dst=dst, t=t: s.copy(
                            qa[gidx][:, 512 * t:512 * (t + 1)], dst[:]), deps=[gm])
                        qev[gidx][t] = ev
                    elif gidx < 8:
                        ev = pr.scalar(lambda s, gidx=gidx, dst=dst, t=t: s.copy(
                            ckv[gidx - 4][:, 512 * t:512 * (t + 1)], dst[:]), deps=[gm])
                        ckev[gidx - 4][t] = ev
                    else:
                        evr_dep = [gm] + ([prev["ropeA"][-1]] if prev["ropeA"][-1] else [])
                        evr = pr.scalar(lambda s, dst=dst: s.copy(rsc[:], dst[0:64, :]),
                                        deps=evr_dep)
                        swm_dep = [evr] + ([prev["ropeB"][-1]] if prev["ropeB"][-1] else [])
                        swm = pr.tensor(lambda te: te.matmul(
                            pswp[:], swpb[:], rsc[:], start=True, stop=True),
                            deps=swm_dep)
                        ra = pr.vector(lambda v, t=t: v.tensor_tensor(
                            xs2[0][0:64, :], rsc[:], rCb[:, 512 * t:512 * (t + 1)],
                            ALU.mult), deps=[evr])
                        rbv = pr.vector(lambda v, t=t: v.tensor_tensor(
                            xs2[1][0:64, :], pswp[:], rSb[:, 512 * t:512 * (t + 1)],
                            ALU.mult), deps=[swm])
                        if gidx < 12:
                            j = gidx - 8
                            dstq = qbp[j // 2]
                            ro = 64 * (j % 2)
                        else:
                            dstq, ro = krope2, 0
                        r3 = pr.vector(lambda v, dstq=dstq, ro=ro, t=t: (
                            v.tensor_tensor(dstq[ro:ro + 32, 512 * t:512 * (t + 1)],
                                            xs2[0][0:32, :], xs2[1][0:32, :],
                                            ALU.subtract),
                            v.tensor_tensor(dstq[ro + 32:ro + 64, 512 * t:512 * (t + 1)],
                                            xs2[0][32:64, :], xs2[1][32:64, :],
                                            ALU.add))[-1], deps=[ra, rbv])
                        prev["ropeA"].append(ra)
                        prev["ropeB"].append(rbv)
                        prev["rope3"] = r3
                        ev = r3
                        if gidx == 12:
                            pr.gpsimd(lambda g, t=t: g.dma_start(
                                out=krope2[64:128, 512 * t:512 * (t + 1)],
                                in_=krope2[0:64, 512 * t:512 * (t + 1)]),
                                deps=[r3], dma=True)
                            krdups.append(pr.steps["gpsimd"][-1])
                    grp_ev.append(ev)
                prev["vst"] = vst
                prev["xln_last"] = grp_ev[12]
                prev["p1last"] = [xlns[NE - 2], xlns[NE - 1]]

            rope_final = prev["rope3"]

            es1.close()
            es2 = ExitStack()
            A2 = lambda n, s, d=F32: es2.enter_context(nc.sbuf_tensor(n, s, d))
            ksb = A2("ksb", [128, SEQ], BF16)
            vT = A2("vT", [128, 32 * 193], BF16)
            ex = [A2(f"ex{i}", [128, 512], BF16) for i in range(2)]
            nrq = A2("nrq", [1, SEQ], BF16); nrk = A2("nrk", [1, SEQ], BF16)
            nmx = A2("nmx", [1, 8])
            negc = A2("negc", [128, 1])
            avev = A2("avev", [128, 512], BF16)
            avev2 = A2("avev2", [65, 512], BF16)
            lev = A2("lev", [65, 512])
            sqn = [A2("sqn0", [128, 512]), A2("sqn1", [64, 512])]
            vt_init = pr.vector(lambda v: (v.memset(vT[:], 0.0),
                                           v.memset(vT[:, 192:32 * 193:193], 1.0))[-1])

            hdr = dict(last_ev=None, psfree=[None, None], last_avev=None)
            avdma = []
            for h in range(HG):
                kevs, vevs = [], []
                for t in range(NCH):
                    dd = [ckev[r][t] for r in range(4)]
                    if len(kevs) >= 2:
                        dd.append(kevs[-2])
                    elif hdr["psfree"][t % 2] is not None:
                        dd.append(hdr["psfree"][t % 2])
                    km = pr.tensor(lambda te, h=h, t=t: [te.matmul(
                        pS[t % 2][:],
                        wupb[:, 1280 * r + 320 * h:1280 * r + 320 * h + 128],
                        ckv[r][:, 512 * t:512 * (t + 1)],
                        start=(r == 0), stop=(r == 3)) for r in range(4)][-1],
                        deps=dd)
                    ke = pr.scalar(lambda s, t=t: s.copy(
                        ksb[:, 512 * t:512 * (t + 1)], pS[t % 2][:]), deps=[km])
                    kevs.append(ke)
                for tl in range(32):
                    dd = [ckev[r][tl // 4] for r in range(4)] + [vt_init]
                    si = tl % 2
                    if len(vevs) >= 2:
                        dd.append(vevs[-2])
                    else:
                        dd.append(kevs[NCH - 2 + tl])
                    vm = pr.tensor(lambda te, h=h, tl=tl, si=si: [te.matmul(
                        pS[si][0:128, 0:192],
                        ckv[r][:, 128 * tl:128 * (tl + 1)],
                        wupb[:, 1280 * r + 320 * h + 128:1280 * r + 320 * h + 320],
                        start=(r == 0), stop=(r == 3)) for r in range(4)][-1],
                        deps=dd)
                    ve = pr.scalar(lambda s, tl=tl, si=si: s.copy(
                        vT[:, 193 * tl:193 * tl + 192], pS[si][0:128, 0:192]),
                        deps=[vm])
                    vevs.append(ve)
                nsteps = []
                for (srcq, dst) in ((True, nrq), (False, nrk)):
                    for t in range(NCH):
                        sqdep = [nsteps[-1]] if nsteps else [vevs[-1]]
                        if srcq:
                            s1 = pr.scalar(lambda s, h=h, t=t: s.activation(
                                out=sqn[0][:], in_=qa[h][:, 512 * t:512 * (t + 1)],
                                func=AF.Square), deps=[qev[h][t]] + sqdep)
                            s2 = pr.scalar(lambda s, h=h, t=t: s.activation(
                                out=sqn[1][:],
                                in_=qbp[h // 2][64 * (h % 2):64 * (h % 2) + 64,
                                                512 * t:512 * (t + 1)],
                                func=AF.Square), deps=[rope_final])
                        else:
                            s1 = pr.scalar(lambda s, t=t: s.activation(
                                out=sqn[0][:], in_=ksb[:, 512 * t:512 * (t + 1)],
                                func=AF.Square), deps=[kevs[t]] + sqdep)
                            s2 = pr.scalar(lambda s, t=t: s.activation(
                                out=sqn[1][:],
                                in_=krope2[0:64, 512 * t:512 * (t + 1)],
                                func=AF.Square), deps=[rope_final])
                        mm1 = pr.tensor(lambda te: te.matmul(
                            pnn[:], ones128f[:], sqn[0][:], start=True, stop=False),
                            deps=[s1] + ([nsteps[-1]] if nsteps else []))
                        mm2 = pr.tensor(lambda te: te.matmul(
                            pnn[:], ones128f[0:64, :], sqn[1][:],
                            start=False, stop=True), deps=[s2])
                        nev = pr.vector(lambda v, dst=dst, t=t: v.tensor_copy(
                            dst[:, 512 * t:512 * (t + 1)], pnn[:]), deps=[mm2])
                        nsteps.append(nev)
                def vmx_emit(v):
                    v.reduce_max(nmx[:, 0:1], nrq[:], AX)
                    edge(v, v.reduce_max(nmx[:, 1:2], nrk[:], AX))
                    return v.tensor_scalar(nmx[:, 3:4], nmx[:, 0:1], nmx[0:1, 1:2],
                                           None, ALU.mult)
                vmx = pr.vector(vmx_emit, deps=[nsteps[-1]])
                ssx = pr.scalar(lambda s: s.activation(
                    out=nmx[:, 4:5], in_=nmx[:, 3:4], func=AF.Sqrt), deps=[vmx])
                cbm = pr.tensor(lambda te: te.matmul(
                    pava[:, 0:1], ones1x[:], nmx[:, 4:5], start=True, stop=True),
                    deps=[ssx] + ([hdr["last_avev"]] if hdr["last_avev"] else []))
                vnc = pr.vector(lambda v: v.tensor_scalar(
                    negc[:], pava[:, 0:1], -1.0, None, ALU.mult), deps=[cbm])
                if d_cdbg is not None:
                    pr.gpsimd(lambda g, h=h: g.dma_start(
                        out=d_cdbg[h:h + 1, :], in_=nmx[:, :]), deps=[vnc], dma=True)
                    if h == 0 and d_nrm is not None:
                        pr.gpsimd(lambda g: g.dma_start(
                            out=d_nrm[0:1, :], in_=nrq[:, :]), deps=[vmx], dma=True)
                        pr.gpsimd(lambda g: g.dma_start(
                            out=d_nrm[1:2, :], in_=nrk[:, :]), deps=[vmx], dma=True)

                sc_prev = [None, None]
                ex_prev = [None, None]
                ev_of_tile = {}
                for qc in range(NCH):
                    nkt = 4 * (qc + 1)
                    avmm = []
                    for kt in range(nkt):
                        si = kt % 2
                        dd = [kevs[kt // 4], krdups[kt // 4], vnc]
                        if sc_prev[si] is not None:
                            dd.append(ev_of_tile[sc_prev[si]])
                        scm = pr.tensor(lambda te, h=h, kt=kt, qc=qc, si=si: (
                            te.matmul(pS[si][:], ksb[:, 128 * kt:128 * (kt + 1)],
                                      qa[h][:, 512 * qc:512 * (qc + 1)],
                                      start=True, stop=False),
                            te.matmul(pS[si][:],
                                      krope2[64 * (h % 2):64 * (h % 2) + 64,
                                             128 * kt:128 * (kt + 1)],
                                      qbp[h // 2][64 * (h % 2):64 * (h % 2) + 64,
                                                  512 * qc:512 * (qc + 1)],
                                      start=False, stop=True))[-1], deps=dd)
                        exdep = [scm]
                        if ex_prev[si] is not None:
                            exdep.append(ex_prev[si])
                        exq = pr.scalar(lambda s, si=si: s.activation(
                            out=ex[si][:], in_=pS[si][:], func=AF.Exp,
                            bias=negc[:, 0:1]), deps=exdep)
                        lastex = exq
                        if kt >= nkt - 4:
                            rel = kt - (nkt - 4)
                            lastex = pr.vector(lambda v, si=si, rel=rel: v.tensor_tensor(
                                ex[si][:], ex[si][:],
                                l0mb[:, 512 * rel:512 * (rel + 1)], ALU.mult),
                                deps=[exq])
                        ev_of_tile[(qc, kt)] = lastex
                        avdep = [lastex, vevs[min(kt // 4, 31) if False else kt]] if False else [lastex, vevs[kt]]
                        if kt == 0:
                            if hdr.get("last_avev"):
                                avdep.append(hdr["last_avev"])
                            avdep.append(vnc)
                        am = pr.tensor(lambda te, kt=kt, si=si, nkt=nkt: (
                            te.matmul(pava[:], vT[:, 193 * kt:193 * kt + 128],
                                      ex[si][:], start=(kt == 0), stop=(kt == nkt - 1),
                                      skip_group_check=True),
                            te.matmul(pavb[0:65, :],
                                      vT[:, 193 * kt + 128:193 * kt + 193],
                                      ex[si][:], start=(kt == 0), stop=(kt == nkt - 1),
                                      skip_group_check=True)
                        )[-1], deps=avdep)
                        avmm.append(am)
                        sc_prev[si] = (qc, kt)
                        ex_prev[si] = am
                    dd = [avmm[-1]]
                    if avdma:
                        dd += avdma[-3:]
                    avev_s = pr.vector(lambda v: (
                        v.tensor_copy(avev[:], pava[:]),
                        v.tensor_copy(avev2[0:64, :], pavb[0:64, :]),
                        v.tensor_copy(lev[64:65, :], pavb[64:65, :]))[-1], deps=dd)
                    hdr["last_avev"] = avev_s
                    d1_ = pr.gpsimd(lambda g, h=h, qc=qc: g.dma_start(
                        out=avd[192 * h:192 * h + 128, 512 * qc:512 * (qc + 1)],
                        in_=avev[:]), deps=[avev_s], dma=True)
                    d2_ = pr.gpsimd(lambda g, h=h, qc=qc: g.dma_start(
                        out=avd[192 * h + 128:192 * h + 192, 512 * qc:512 * (qc + 1)],
                        in_=avev2[0:64, :]), deps=[avev_s], dma=True)
                    d3_ = pr.gpsimd(lambda g, h=h, qc=qc: g.dma_start(
                        out=ldm[h:h + 1, 512 * qc:512 * (qc + 1)],
                        in_=lev[64:65, :]), deps=[avev_s], dma=True)
                    avdma += [d1_, d2_, d3_]
                hdr["psfree"] = [ev_of_tile[sc_prev[0]], ev_of_tile[sc_prev[1]]]
                hdr["last_ev"] = vevs[-1]

            if d_q is not None:
                for j in range(HG):
                    pr.gpsimd(lambda g, j=j: g.dma_start(
                        out=d_q[192 * j:192 * j + 128, :], in_=qa[j][:]),
                        deps=[qev[j][NCH - 1]], dma=True)
                    pr.gpsimd(lambda g, j=j: g.dma_start(
                        out=d_q[192 * j + 128:192 * j + 192, :],
                        in_=qbp[j // 2][64 * (j % 2):64 * (j % 2) + 64, :]),
                        deps=[pr.steps["vector"][-1]], dma=True)
            if d_ckv is not None:
                for r in range(4):
                    pr.gpsimd(lambda g, r=r: g.dma_start(
                        out=d_ckv[128 * r:128 * (r + 1), :], in_=ckv[r][:]),
                        deps=[ckev[r][NCH - 1]], dma=True)
                pr.gpsimd(lambda g: g.dma_start(out=d_ckv[512:640, :], in_=krope2[:]),
                          deps=[pr.steps["gpsimd"][-1]], dma=True)
            if d_av is not None:
                pr.gpsimd(lambda g: g.dma_start(out=d_av[:, :], in_=avd[:, :]),
                          deps=avdma[-3:], dma=True)
            if d_l is not None:
                pr.gpsimd(lambda g: g.dma_start(out=d_l[:, :], in_=ldm[:, :]),
                          deps=avdma[-3:] + [pr.steps["gpsimd"][-1]], dma=True)

            pr.selfwait("gpsimd")
            pr.barrier()
            pr.compile(nc, ges)
            es2.close()

        if upto <= 2:
            _fin(nc, ges)
            return nc, dbg_out

        _phaseB(nc, ges, locals())
        if upto <= 3:
            _fin(nc, ges)
            return nc, dbg_out

        _phaseC(nc, ges, locals())
        if upto <= 4:
            _fin(nc, ges)
            return nc, dbg_out

        _phaseD(nc, ges, locals())
        if upto <= 5:
            _fin(nc, ges)
            return nc, dbg_out

        _phaseE(nc, ges, locals())

        _fin(nc, ges)
    return nc, dbg_out


def _ln_finalize(pr, nc, edge, deps, pA, pB, ntok, stv, o_r, o_mr, invn=1.0 / D,
                 scale=1.0):
    AFl, ALUl = AF, ALU
    N = ntok

    def emit_v1(v):
        v.tensor_scalar(stv[:, 0:N], pA[0:1, 0:N], invn, None, ALUl.mult)
        v.tensor_scalar(stv[:, 1024:1024 + N], pB[0:1, 0:N], invn, EPS,
                        ALUl.mult, ALUl.add)
        v.tensor_tensor(stv[:, 512:512 + N], stv[:, 0:N], stv[:, 0:N], ALUl.mult)
        return v.tensor_tensor(stv[:, 1024:1024 + N], stv[:, 1024:1024 + N],
                               stv[:, 512:512 + N], ALUl.subtract)
    vst = pr.vector(emit_v1, deps=deps)
    ssq = pr.scalar(lambda s: s.activation(out=stv[:, 1536:1536 + N],
                                           in_=stv[:, 1024:1024 + N], func=AFl.Sqrt),
                    deps=[vst])

    def emit_v2(v):
        edge(v, v.reciprocal(stv[:, 512:512 + N], stv[:, 1536:1536 + N]))
        v.tensor_tensor(stv[:, 1536:1536 + N], stv[:, 1024:1024 + N],
                        stv[:, 512:512 + N], ALUl.mult)
        v.tensor_tensor(stv[:, 1536:1536 + N], stv[:, 1536:1536 + N],
                        stv[:, 512:512 + N], ALUl.mult)
        v.tensor_scalar(stv[:, 1536:1536 + N], stv[:, 1536:1536 + N], -0.5, 1.5,
                        ALUl.mult, ALUl.add)
        v.tensor_tensor(stv[:, 512:512 + N], stv[:, 512:512 + N],
                        stv[:, 1536:1536 + N], ALUl.mult)
        if scale != 1.0:
            v.tensor_scalar(stv[:, 512:512 + N], stv[:, 512:512 + N],
                            scale, None, ALUl.mult)
        v.tensor_copy(o_r, stv[:, 512:512 + N])
        return v.tensor_tensor(o_mr, stv[:, 0:N], stv[:, 512:512 + N], ALUl.mult)
    return pr.vector(emit_v2, deps=[ssq])


def _mkconds(nc, engmap, eng_obj, ename):
    if ename not in engmap:
        et = {"gpsimd": ET.Pool, "scalar": ET.Activation, "sync": ET.SP}[ename]
        rid = nc.partition_id(engines=[et])
        engmap[ename] = [eng_obj.compute_val(rid == r) for r in range(8)]
    return engmap[ename]


def _phaseB(nc, ges, L):
    g = lambda n: L[n]
    avd, ldm, avg, lg, x1d, xresT = g("avd"), g("ldm"), g("avg"), g("lg"), g("x1d"), g("xresT")
    wouts, d_x1, mub = g("wouts"), g("d_x1"), g("mub")
    d_avgf, d_lgf = g("d_avgf"), g("d_lgf")
    d_avgs, d_lsbf, d_aof = g("d_avgs"), g("d_lsbf"), g("d_aof")
    d_avgp = g("d_avgp")
    ones128f, ones1x = g("ones128f"), g("ones1x")
    edge = g("edge")
    esB = ExitStack()
    A = lambda n, s, d=F32: esB.enter_context(nc.sbuf_tensor(n, s, d))
    PS = lambda n: esB.enter_context(nc.psum_tensor(n, [128, 512], F32))
    with esB:
        avgs = A("avgs", [128, 24 * TOK], BF16)
        lsb = A("lsb", [16, TOK])
        woutb = [A(f"woutb{i}", [128, 3072], BF16) for i in range(2)]
        ao = A("ao", [128, NE * TOK])
        sqB = [A(f"sqB{i}", [128, 512]) for i in range(2)]
        stvB = A("stvB", [1, 4 * 512])
        rmv = A("rmv", [1, TOK])
        mrv = A("mrv", [1, TOK])
        xr = [A(f"xr{i}", [128, TOK]) for i in range(2)]
        x1t = [A(f"x1t{i}", [128, TOK]) for i in range(2)]
        P = [PS(f"PB{i}") for i in range(8)]
        TT3 = [(0, 512), (512, 512), (1024, TOK - 1024)]
        HOFF = [max(0, 1024 * (r % 4) - 128) for r in range(8)]
        OOFF = [1024 * (r % 4) for r in range(8)]

        mubb = A("mubb", [128, 4])
        tmpo = [A(f"tmpo{i}", [128, 1152], BF16) for i in range(2)]
        tmol = [A(f"tmol{i}", [16, 1152]) for i in range(2)]

        prcc = Prog("pBcc")
        prcc.add("gpsimd", lambda gp, sem: gp.collective_compute(
            "AllGather", ALU.bypass, replica_groups=[[0, 1, 2, 3], [4, 5, 6, 7]],
            ins=[avd.opt()], outs=[avg.opt()]).then_inc(sem, 1), w=1)
        prcc.add("gpsimd", lambda gp, sem: gp.collective_compute(
            "AllGather", ALU.bypass, replica_groups=[[0, 1, 2, 3], [4, 5, 6, 7]],
            ins=[ldm.opt()], outs=[lg.opt()]).then_inc(sem, 1), w=1)
        prcc.selfwait("gpsimd")
        prcc.compile(nc, ges)

        pr = Prog("pB")
        cmap = {}
        if d_avgf is not None:
            pr.gpsimd(lambda gp: gp.dma_start(out=d_avgf[:, :], in_=avg[:, :]),
                      dma=True)
        if d_lgf is not None:
            pr.gpsimd(lambda gp: gp.dma_start(out=d_lgf[:, :], in_=lg[:, :]),
                      dma=True)
        muld = pr.sync(lambda sy: sy.dma_start(out=mubb[:], in_=mub[:, :]))
        agy = None
        agl = None
        WOFF = [0, 896, 1920, 2944]
        stat, sl_fc, sl_l = [], [], []
        for fc in range(24):
            stat.append(pr.gpsimd(lambda gp, fc=fc: gp.dma_start(
                out=avgs[:, TOK * fc:TOK * fc + 16],
                in_=avg[128 * fc:128 * (fc + 1), 0:16]), deps=[agy], dma=True))
            last = None
            for j in range(4):
                dd = [agy] + ([sl_fc[-2]] if len(sl_fc) >= 2 else [])
                ld = pr.gpsimd(lambda gp, fc=fc, j=j: gp.dma_start(
                    out=tmpo[j % 2][:],
                    in_=avg[128 * fc:128 * (fc + 1), WOFF[j]:WOFF[j] + 1152]),
                    deps=dd, dma=True)
                def emit(v, fc=fc, j=j):
                    if j == 0:
                        v.tensor_scalar(avgs[:, TOK * fc + 16:TOK * fc + 144],
                                        tmpo[0][:, 0:128], mubb[:, 0:1], None,
                                        ALU.mult)
                        return v.tensor_scalar(avgs[:, TOK * fc + 144:TOK * (fc + 1)],
                                              tmpo[0][:, 0:1024], mubb[:, 0:1], None,
                                              ALU.mult)
                    return v.scalar_tensor_tensor(
                        avgs[:, TOK * fc + 16:TOK * (fc + 1)], tmpo[j % 2][:],
                        mubb[:, j:j + 1], avgs[:, TOK * fc + 16:TOK * (fc + 1)],
                        ALU.mult, ALU.add)
                last = pr.vector(emit, deps=[ld, muld])
                sl_fc.append(last)
            stat.append(last)
        stat.append(pr.gpsimd(lambda gp: gp.dma_start(
            out=lsb[:, 0:16], in_=lg[:, 0:16]), deps=[agl], dma=True))
        for j in range(4):
            dd = [agl] + ([sl_l[-2]] if len(sl_l) >= 2 else [])
            ld = pr.gpsimd(lambda gp, j=j: gp.dma_start(
                out=tmol[j % 2][:], in_=lg[:, WOFF[j]:WOFF[j] + 1152]),
                deps=dd, dma=True)
            def emitl(v, j=j):
                if j == 0:
                    v.tensor_scalar(lsb[:, 16:144], tmol[0][:, 0:128],
                                    mubb[0:16, 0:1], None, ALU.mult)
                    return v.tensor_scalar(lsb[:, 144:TOK], tmol[0][:, 0:1024],
                                          mubb[0:16, 0:1], None, ALU.mult)
                return v.scalar_tensor_tensor(
                    lsb[:, 16:TOK], tmol[j % 2][:], mubb[0:16, j:j + 1],
                    lsb[:, 16:TOK], ALU.mult, ALU.add)
            sl_l.append(pr.vector(emitl, deps=[ld, muld]))
        stat.append(sl_l[-1])
        slc_all = stat
        lslc = slc_all
        slc = None
        vli = pr.vector(lambda v: v.reciprocal(lsb[:], lsb[:]), deps=lslc)
        nunit = 0
        prevtt = {}
        nrmsteps = {}
        lrow = esB.enter_context(nc.sbuf_tensor("lrow", [1, TOK], F32))
        lrowdma = None
        for h in range(H):
            pieces = []
            r0, r1 = 192 * h, 192 * (h + 1)
            fc = r0 // 128
            while 128 * fc < r1:
                a = max(r0, 128 * fc)
                bnd = min(r1, 128 * (fc + 1))
                pieces.append((fc, a - 128 * fc, bnd - a))
                fc += 1
            dd = [vli]
            if lrowdma is not None:
                dd.append(prevtt[nunit - 1])
            lrowdma = pr.gpsimd(lambda gp, h=h: gp.dma_start(
                out=lrow[:], in_=lsb[h:h + 1, :]), deps=dd, dma=True)
            for ti, (to, tn) in enumerate(TT3):
                bank = P[nunit % 2]
                dd = [lrowdma]
                if nunit >= 2:
                    dd.append(prevtt[nunit - 2])
                bc = pr.tensor(lambda te, to=to, tn=tn, bank=bank: te.matmul(
                    bank[:, 0:tn], ones1x[:], lrow[0:1, to:to + tn],
                    start=True, stop=True), deps=dd)
                def ntt(v, pieces=pieces, to=to, tn=tn, bank=bank):
                    for (fc, off, ln) in pieces:
                        last = v.tensor_tensor(
                            avgs[off:off + ln, TOK * fc + to:TOK * fc + to + tn],
                            avgs[off:off + ln, TOK * fc + to:TOK * fc + to + tn],
                            bank[off:off + ln, 0:tn], ALU.mult)
                    return last
                tt = pr.vector(ntt, deps=[bc] + slc_all)
                for (fc, off, ln) in pieces:
                    nrmsteps[(fc, ti, h)] = tt
                prevtt[nunit] = tt
                nunit += 1
        if d_avgp is not None:
            pr.gpsimd(lambda gp: gp.dma_start(out=d_avgp[:, :], in_=avgs[:]),
                      deps=slc_all, dma=True)
        if d_lsbf is not None:
            pr.gpsimd(lambda gp: gp.dma_start(out=d_lsbf[:, :], in_=lsb[:]),
                      deps=[vli], dma=True)
        if d_avgs is not None:
            pr.gpsimd(lambda gp: gp.dma_start(out=d_avgs[:, :], in_=avgs[:]),
                      deps=[prevtt[nunit - 1]], dma=True)

        def nsteps_for(fc, ti):
            return [v for (f2, t2, h2), v in nrmsteps.items() if f2 == fc and t2 == ti]
        evs = {}
        stt_mm = {}
        for ec in range(NE):
            lw = pr.sync(lambda sy, ec=ec: sy.dma_start(
                out=woutb[ec % 2][:].rearrange("p (c j) -> p c j", j=128),
                in_=wouts[ec].rearrange("(c p) j -> p c j", p=128)),
                deps=([evs[3 * ec - 4]] if ec >= 2 else []))
            for ti, (to, tn) in enumerate(TT3):
                u = 3 * ec + ti
                bank = P[u % 2]
                dd = [lw] + [st for fc2 in range(24) for st in nsteps_for(fc2, ti)]
                if u >= 2:
                    dd.append(evs[u - 2])
                if u < 2:
                    dd.append(prevtt[nunit - 2 + u])
                def mmo(te, to=to, tn=tn, bank=bank, ec=ec):
                    last = None
                    for fc in range(24):
                        last = te.matmul(bank[:, 0:tn],
                                         woutb[ec % 2][:, 128 * fc:128 * (fc + 1)],
                                         avgs[:, TOK * fc + to:TOK * fc + to + tn],
                                         start=(fc == 0), stop=(fc == 23))
                    return last
                mo = pr.tensor(mmo, deps=dd)
                ev = pr.scalar(lambda s, ec=ec, to=to, tn=tn, bank=bank: s.copy(
                    ao[:, TOK * ec + to:TOK * ec + to + tn], bank[:, 0:tn]), deps=[mo])
                evs[u] = ev
                sqs = pr.scalar(lambda s, ec=ec, to=to, tn=tn, u=u: s.activation(
                    out=sqB[u % 2][:, 0:tn], in_=ao[:, TOK * ec + to:TOK * ec + to + tn],
                    func=AF.Square), deps=[ev] + ([stt_mm[(u - 2, 1)]] if (u - 2, 1) in stt_mm else []))
                m1 = pr.tensor(lambda te, ti=ti, to=to, tn=tn, ec=ec: te.matmul(
                    P[2 + ti][0:1, 0:tn], ones128f[:],
                    ao[:, TOK * ec + to:TOK * ec + to + tn],
                    start=(ec == 0), stop=(ec == NE - 1), skip_group_check=True),
                    deps=[ev])
                m2 = pr.tensor(lambda te, ti=ti, tn=tn, u=u: te.matmul(
                    P[5 + ti][0:1, 0:tn], ones128f[:], sqB[u % 2][:, 0:tn],
                    start=(u // 3 == 0), stop=(u // 3 == NE - 1), skip_group_check=True),
                    deps=[sqs])
                stt_mm[(u, 0)] = m1
                stt_mm[(u, 1)] = m2
        if d_aof is not None:
            pr.gpsimd(lambda gp: gp.dma_start(out=d_aof[:, :], in_=ao[:]),
                      deps=[evs[3 * NE - 1]], dma=True)
        fins = []
        for ti, (to, tn) in enumerate(TT3):
            dd = [stt_mm[(3 * (NE - 1) + ti, 0)], stt_mm[(3 * (NE - 1) + ti, 1)]]
            if fins:
                dd.append(fins[-1])
            fins.append(_ln_finalize(pr, nc, edge, dd, P[2 + ti], P[5 + ti], tn,
                                     stvB, rmv[:, 512 * ti:512 * ti + tn],
                                     mrv[:, 512 * ti:512 * ti + tn]))
        bcs = []
        for ti, (to, tn) in enumerate(TT3):
            bc = pr.tensor(lambda te, ti=ti, tn=tn: (
                te.matmul(P[2 + ti][:, 0:tn], ones1x[:], rmv[:, 512 * ti:512 * ti + tn],
                          start=True, stop=True),
                te.matmul(P[5 + ti][:, 0:tn], ones1x[:], mrv[:, 512 * ti:512 * ti + tn],
                          start=True, stop=True))[-1], deps=[fins[ti], fins[-1]])
            bcs.append(bc)
        x1dma = []
        for c in range(NE):
            lddep = [x1dma[c - 2]] if c >= 2 else []
            lx = pr.sync(lambda sy, c=c: sy.dma_start(
                out=xr[c % 2][:], in_=xresT[128 * c:128 * (c + 1), :]), deps=lddep)
            def resid(v, c=c):
                last = None
                for ti, (to, tn) in enumerate(TT3):
                    v.tensor_tensor(x1t[c % 2][:, to:to + tn],
                                    ao[:, TOK * c + to:TOK * c + to + tn],
                                    P[2 + ti][:, 0:tn], ALU.mult)
                    v.tensor_tensor(x1t[c % 2][:, to:to + tn], x1t[c % 2][:, to:to + tn],
                                    P[5 + ti][:, 0:tn], ALU.subtract)
                    last = v.tensor_tensor(x1t[c % 2][:, to:to + tn],
                                           x1t[c % 2][:, to:to + tn],
                                           xr[c % 2][:, to:to + tn], ALU.add)
                return last
            rv = pr.vector(resid, deps=[lx, evs[3 * c], evs[3 * c + 1], evs[3 * c + 2]] + bcs
                           + ([x1dma[c - 2]] if c >= 2 else []))
            dm = pr.gpsimd(lambda gp, c=c: gp.dma_start(
                out=x1d[128 * c:128 * (c + 1), :], in_=x1t[c % 2][:]), deps=[rv], dma=True)
            x1dma.append(dm)
        if d_x1 is not None:
            for c in range(NE):
                pr.gpsimd(lambda gp, c=c: gp.dma_start(
                    out=d_x1[128 * c:128 * (c + 1), :],
                    in_=x1d[128 * c:128 * (c + 1), :]), deps=[x1dma[c]], dma=True)
        pr.barrier()
        pr.compile(nc, ges)



def _fin(nc, ges):
    pr = Prog("fin")
    for e in ("sync", "scalar", "vector", "tensor", "gpsimd"):
        pr.add(e, lambda eng: eng.nop())
    pr.barrier()
    pr.compile(nc, ges)


def _phaseC(nc, ges, L):
    g = lambda n: L[n]
    w1s, w2s, b1p_, b2p_ = g("w1s"), g("w2s"), g("b1p"), g("b2p")
    x1d, x2d, d_x2 = g("x1d"), g("x2d"), g("d_x2")
    d_cthr, d_cxln, d_chid = g("d_cthr"), g("d_cxln"), g("d_chid")
    d_cstat = g("d_cstat")
    ones128f, ones1x, edge = g("ones128f"), g("ones1x"), g("edge")
    FH = TOK // 2
    TT2 = [(0, 512), (512, FH - 512)]
    esC = ExitStack()
    A = lambda n, s, d=F32: esC.enter_context(nc.sbuf_tensor(n, s, d))
    PS = lambda n: esC.enter_context(nc.psum_tensor(n, [128, 512], F32))
    with esC:
        redb = A("redb", [128, 132])
        thr = A("thr", [128, 4])
        tsb = A("tsb", [1, 2])
        b1s = A("b1s", [128, NH1]); b2s = A("b2s", [128, NE])
        wblk = [A(f"wblk{i}", [128, 2048]) for i in range(2)]
        s1p = A("s1p", [128, 2048], BF16)
        s2p = A("s2p", [128, 2048], BF16)
        w2blk = A("w2blk", [128, 4096])
        s1w = A("s1w", [128, 4096], BF16)
        s2w = A("s2w", [128, 4096], BF16)
        xs = [A(f"xsC{i}", [128, FH]) for i in range(3)]
        xln3 = A("xln3", [128, NE * FH], BF16)
        hid = A("hid", [128, NH1 * FH], BF16)
        ffno = A("ffno", [128, NE * FH])
        _sqC = A("sqC0", [128, 512])
        sqC = [_sqC, _sqC]
        stvC = A("stvC", [1, 4 * 512])
        rmvC = A("rmvC", [1, 640])
        mrvC = A("mrvC", [1, 640])
        P = [PS(f"PC{i}") for i in range(8)]
        pr = Prog("pC")

        bld = pr.sync(lambda sy: sy.dma_start(out=b1s[:], in_=b1p_[:, :]))
        bld2 = pr.sync(lambda sy: sy.dma_start(out=b2s[:], in_=b2p_[:, :]))

        red_steps = []
        units = [("w1", i) for i in range(NH1)] + [("w2", i) for i in range(64)]
        for col, (kind, idx) in enumerate(units):
            dd = [red_steps[-2]] if len(red_steps) >= 2 else []
            if kind == "w1":
                ld = pr.sync(lambda sy, idx=idx, col=col: sy.dma_start(
                    out=wblk[col % 2][:].rearrange("p (c j) -> p c j", j=128),
                    in_=w1s[idx].rearrange("(c p) j -> p c j", p=128)), deps=dd)
            else:
                ec, qq = idx // 4, idx % 4
                ld = pr.sync(lambda sy, ec=ec, qq=qq, col=col: sy.dma_start(
                    out=wblk[col % 2][:].rearrange("p (c j) -> p c j", j=128),
                    in_=w2s[ec, 2048 * qq:2048 * (qq + 1), :]
                        .rearrange("(c p) j -> p c j", p=128)), deps=dd)
            rd = pr.vector(lambda v, col=col: v.tensor_reduce(
                redb[:, col:col + 1], wblk[col % 2][:], AX, ALU.add,
                apply_absolute_value=True), deps=[ld])
            red_steps.append(rd)
        rtot = pr.vector(lambda v: (
            v.tensor_reduce(redb[:, 128:129], redb[:, 0:64], AX, ALU.add),
            v.tensor_reduce(redb[:, 129:130], redb[:, 64:128], AX, ALU.add))[-1],
            deps=[red_steps[-1]])
        smm = pr.tensor(lambda te: te.matmul(P[0][0:1, 0:2], ones128f[:],
                                             redb[:, 128:130], start=True, stop=True),
                        deps=[rtot])
        vth = pr.vector(lambda v: v.tensor_scalar(
            tsb[:], P[0][0:1, 0:2], 0.7 / (DFF * D), None, ALU.mult), deps=[smm])
        bth = pr.tensor(lambda te: (
            te.matmul(P[0][:, 0:1], ones1x[:], tsb[0:1, 0:1], start=True, stop=True),
            te.matmul(P[1][:, 0:1], ones1x[:], tsb[0:1, 1:2], start=True, stop=True))[-1],
            deps=[vth])
        vthr = pr.vector(lambda v: (
            v.tensor_scalar(thr[:, 0:1], P[0][:, 0:1], -1.0, None, ALU.mult),
            v.tensor_scalar(thr[:, 1:2], P[0][:, 0:1], 1.0, None, ALU.mult),
            v.tensor_scalar(thr[:, 2:3], P[1][:, 0:1], -1.0, None, ALU.mult),
            v.tensor_scalar(thr[:, 3:4], P[1][:, 0:1], 1.0, None, ALU.mult))[-1],
            deps=[bth])
        if d_cthr is not None:
            pr.gpsimd(lambda gp: gp.dma_start(out=d_cthr[:, :], in_=thr[:]),
                      deps=[vthr], dma=True)

        x2dmas = []
        prev_half = {}
        for u in range(2):
            co = FH * u
            lds, sqs, mmAs, mmBs = [], [], [], []
            for c in range(NE):
                if c >= 2:
                    dd = [mmAs[c - 2], sqs[c - 2]]
                elif prev_half.get("xfree"):
                    dd = list(prev_half["xfree"]) + prev_half["hdm"][-2:]
                else:
                    dd = []
                dd = list(dd) + ([vthr] if c == 0 and u == 0 else [])
                ld = pr.sync(lambda sy, c=c, co=co: sy.dma_start(
                    out=xs[c % 2][:], in_=x1d[128 * c:128 * (c + 1), co:co + FH]),
                    deps=dd)
                lds.append(ld)
                d0 = [ld] + ([prev_half["fin3"]] if c == 0 and prev_half.get("fin3") else [])
                def mA(te, c=c):
                    te.matmul(P[2][0:1, 0:512], ones128f[:], xs[c % 2][:, 0:512],
                              start=(c == 0), stop=(c == NE - 1), skip_group_check=True)
                    return te.matmul(P[3][0:1, 0:FH - 512], ones128f[:], xs[c % 2][:, 512:FH],
                                     start=(c == 0), stop=(c == NE - 1), skip_group_check=True)
                mmAs.append(pr.tensor(mA, deps=d0))
                sq_ = pr.scalar(lambda s, c=c: s.activation(
                    out=sqC[c % 2][:, 0:FH - 0] if False else None, in_=None, func=None) if False else s.activation(
                    out=xs[2][:], in_=xs[c % 2][:], func=AF.Square), deps=[ld] +
                    ([mmBs[c - 1]] if c >= 1 else ([prev_half["fin3"]] if prev_half.get("fin3") else [])))
                sqs.append(sq_)
                def mB(te, c=c):
                    te.matmul(P[4][0:1, 0:512], ones128f[:], xs[2][:, 0:512],
                              start=(c == 0), stop=(c == NE - 1), skip_group_check=True)
                    return te.matmul(P[5][0:1, 0:FH - 512], ones128f[:], xs[2][:, 512:FH],
                                     start=(c == 0), stop=(c == NE - 1), skip_group_check=True)
                mmBs.append(pr.tensor(mB, deps=[sq_]))
            if d_cstat is not None and u == 0:
                stdbg = xs[2]
                cpd = pr.scalar(lambda s: (
                    s.copy(stdbg[0:1, 0:512], P[2][0:1, 0:512]),
                    s.copy(stdbg[0:1, 512:FH], P[3][0:1, 0:FH - 512]))[-1],
                    deps=[mmAs[-1], mmBs[-1]])
                dmp1 = pr.gpsimd(lambda gp: gp.dma_start(
                    out=d_cstat[0:1, :], in_=stdbg[0:1, 0:FH]),
                    deps=[cpd], dma=True)
                cpd2 = pr.scalar(lambda s: (
                    s.copy(stdbg[0:1, 0:512], P[4][0:1, 0:512]),
                    s.copy(stdbg[0:1, 512:FH], P[5][0:1, 0:FH - 512]))[-1],
                    deps=[dmp1])
                pr.gpsimd(lambda gp: gp.dma_start(
                    out=d_cstat[1:2, :], in_=stdbg[0:1, 0:FH]),
                    deps=[cpd2], dma=True)
            f0 = _ln_finalize(pr, nc, edge, [mmAs[-1], mmBs[-1]], P[2], P[4], 512,
                              stvC, rmvC[:, 0:512], mrvC[:, 0:512], scale=0.5)
            f1 = _ln_finalize(pr, nc, edge, [mmAs[-1], mmBs[-1], f0], P[3], P[5],
                              FH - 512, stvC, rmvC[:, 512:512 + FH - 512],
                              mrvC[:, 512:512 + FH - 512], scale=0.5)
            bc3 = pr.tensor(lambda te: (
                te.matmul(P[2][:, 0:512], ones1x[:], rmvC[:, 0:512], start=True, stop=True),
                te.matmul(P[3][:, 0:FH - 512], ones1x[:], rmvC[:, 512:512 + FH - 512],
                          start=True, stop=True),
                te.matmul(P[4][:, 0:512], ones1x[:], mrvC[:, 0:512], start=True, stop=True),
                te.matmul(P[5][:, 0:FH - 512], ones1x[:], mrvC[:, 512:512 + FH - 512],
                          start=True, stop=True))[-1], deps=[f1])
            xlns = []
            for c in range(NE):
                dd = [xlns[c - 2]] if c >= 2 else [sqs[NE - 2 + c], mmAs[NE - 2 + c]]
                ld = pr.sync(lambda sy, c=c, co=co: sy.dma_start(
                    out=xs[c % 2][:], in_=x1d[128 * c:128 * (c + 1), co:co + FH]),
                    deps=dd + ([mmBs[-1]] if c < 2 else []))
                def xl(v, c=c):
                    for ti, (to, tn) in enumerate(TT2):
                        v.tensor_tensor(sqC[c % 2][:, 0:tn], xs[c % 2][:, to:to + tn],
                                        P[2 + ti][:, 0:tn], ALU.mult)
                        last = v.tensor_tensor(xln3[:, FH * c + to:FH * c + to + tn],
                                               sqC[c % 2][:, 0:tn], P[4 + ti][:, 0:tn],
                                               ALU.subtract)
                    return last
                xlns.append(pr.vector(xl, deps=[ld, bc3] + ([xlns[c - 2]] if c >= 2 else [])))
            if d_cxln is not None and u == 0:
                pr.gpsimd(lambda gp: gp.dma_start(out=d_cxln[:, :], in_=xln3[:]),
                          deps=xlns[-2:], dma=True)
            gels = []
            w1qs = []
            mm1s = []
            for hc in range(NH1):
                dd = [w1qs[hc - 2]] if hc >= 2 else ([vthr] if u == 0 else [prev_half["lastg"]])
                ld = pr.sync(lambda sy, hc=hc: sy.dma_start(
                    out=wblk[hc % 2][:].rearrange("p (c j) -> p c j", j=128),
                    in_=w1s[hc].rearrange("(c p) j -> p c j", p=128)),
                    deps=dd + ([red_steps[-1]] if hc < 2 and u == 0 else []))
                sg1 = pr.scalar(lambda s, hc=hc: s.activation(
                    out=s1p[:], in_=wblk[hc % 2][:], func=AF.Sign, bias=thr[:, 0:1]),
                    deps=[ld] + ([mm1s[hc - 1]] if hc >= 1 else ([vthr] if u == 0 else [prev_half["lastg"]])))
                sg2 = pr.scalar(lambda s, hc=hc: s.activation(
                    out=s2p[:], in_=wblk[hc % 2][:], func=AF.Sign, bias=thr[:, 1:2]),
                    deps=[ld])
                w1q = pr.vector(lambda v: v.tensor_tensor(s1p[:], s1p[:], s2p[:], ALU.add),
                                deps=[sg1, sg2])
                w1qs.append(w1q)
                pb = [P[6], P[7]] if hc % 2 == 0 else [P[2], P[3]]
                dd2 = [w1q, xlns[-1]]
                if hc >= 2:
                    dd2.append(gels[hc - 2])
                else:
                    dd2.append(bc3)
                def m1(te, pb=pb):
                    for ti, (to, tn) in enumerate(TT2):
                        for c in range(NE):
                            last = te.matmul(pb[ti][:, 0:tn], s1p[:, 128 * c:128 * (c + 1)],
                                             xln3[:, FH * c + to:FH * c + to + tn],
                                             start=(c == 0), stop=(c == NE - 1),
                                             skip_group_check=True)
                    return last
                mm = pr.tensor(m1, deps=dd2)
                mm1s.append(mm)
                ge = pr.scalar(lambda s, hc=hc, pb=pb: (
                    s.activation(out=hid[:, FH * hc:FH * hc + 512], in_=pb[0][:, 0:512],
                                 func=AF.Gelu, bias=b1s[:, hc:hc + 1]),
                    s.activation(out=hid[:, FH * hc + 512:FH * (hc + 1)],
                                 in_=pb[1][:, 0:FH - 512],
                                 func=AF.Gelu, bias=b1s[:, hc:hc + 1]))[-1],
                    deps=[mm, bld])
                gels.append(ge)
            if d_chid is not None and u == 0:
                pr.gpsimd(lambda gp: gp.dma_start(out=d_chid[:, :], in_=hid[:]),
                          deps=gels[-2:], dma=True)
            evC, stmm = {}, {}
            w2qs = []
            m2prev = [None]
            for ec in range(NE):
                halves = []
                for q in range(2):
                    widx = 2 * ec + q
                    dd = [w2qs[widx - 1]] if w2qs else [gels[1]]
                    ld = pr.sync(lambda sy, ec=ec, q=q: sy.dma_start(
                        out=w2blk[:].rearrange("p (c j) -> p c j", j=128),
                        in_=w2s[ec, 4096 * q:4096 * (q + 1), :]
                            .rearrange("(c p) j -> p c j", p=128)), deps=dd)
                    sg1 = pr.scalar(lambda s: s.activation(
                        out=s1w[:], in_=w2blk[:], func=AF.Sign, bias=thr[:, 2:3]),
                        deps=[ld] + ([m2prev[0]] if m2prev[0] is not None else []))
                    sg2 = pr.scalar(lambda s: s.activation(
                        out=s2w[:], in_=w2blk[:], func=AF.Sign, bias=thr[:, 3:4]),
                        deps=[ld])
                    w2q = pr.vector(lambda v: v.tensor_tensor(s1w[:], s1w[:], s2w[:],
                                                              ALU.add), deps=[sg1, sg2])
                    w2qs.append(w2q)
                    def m2(te, q=q, ec=ec):
                        for ti, (to, tn) in enumerate(TT2):
                            for j in range(32):
                                hh = 32 * q + j
                                last = te.matmul(P[6 + ti][:, 0:tn],
                                                 s1w[:, 128 * j:128 * (j + 1)],
                                                 hid[:, FH * hh + to:FH * hh + to + tn],
                                                 start=(hh == 0), stop=(hh == NH1 - 1),
                                                 skip_group_check=True)
                        return last
                    mmdep = [w2q, gels[-1]]
                    if ec == 0 and q == 0:
                        mmdep.append(gels[-1])
                    if (ec, q) != (0, 0):
                        pass
                    if ec >= 1 and q == 0:
                        mmdep.append(evC[(ec - 1, 1)])
                    h_ = pr.tensor(m2, deps=mmdep)
                    halves.append(h_)
                    m2prev[0] = h_
                ev = pr.scalar(lambda s, ec=ec: (
                    s.copy(ffno[:, FH * ec:FH * ec + 512], P[6][:, 0:512]),
                    s.copy(ffno[:, FH * ec + 512:FH * (ec + 1)], P[7][:, 0:FH - 512]))[-1],
                    deps=[halves[-1]])
                evC[(ec, 1)] = ev
            sb2 = []
            for ec in range(NE):
                stp = pr.vector(lambda v, ec=ec: v.tensor_scalar(
                    ffno[:, FH * ec:FH * (ec + 1)], ffno[:, FH * ec:FH * (ec + 1)],
                    0.5, b2s[:, ec:ec + 1], ALU.mult, ALU.add), deps=[evC[(ec, 1)], bld2])
                sb2.append(stp)
            mmA4, mmB4, sq4 = [], [], []
            for ec in range(NE):
                def mA4(te, ec=ec):
                    te.matmul(P[2][0:1, 0:512], ones128f[:], ffno[:, FH * ec:FH * ec + 512],
                              start=(ec == 0), stop=(ec == NE - 1), skip_group_check=True)
                    return te.matmul(P[3][0:1, 0:FH - 512], ones128f[:],
                                     ffno[:, FH * ec + 512:FH * (ec + 1)],
                                     start=(ec == 0), stop=(ec == NE - 1),
                                     skip_group_check=True)
                dda = [sb2[ec]]
                if ec == 0:
                    dda.append(bc3)
                    dda.append(xlns[-1])
                mmA4.append(pr.tensor(mA4, deps=dda))
                sq_ = pr.scalar(lambda s, ec=ec: s.activation(
                    out=xs[2][:], in_=ffno[:, FH * ec:FH * (ec + 1)], func=AF.Square),
                    deps=[sb2[ec]] + ([mmB4[ec - 1]] if ec >= 1 else []))
                sq4.append(sq_)
                def mB4(te, ec=ec):
                    te.matmul(P[4][0:1, 0:512], ones128f[:], xs[2][:, 0:512],
                              start=(ec == 0), stop=(ec == NE - 1), skip_group_check=True)
                    return te.matmul(P[5][0:1, 0:FH - 512], ones128f[:], xs[2][:, 512:FH],
                                     start=(ec == 0), stop=(ec == NE - 1),
                                     skip_group_check=True)
                mmB4.append(pr.tensor(mB4, deps=[sq_]))
            f0 = _ln_finalize(pr, nc, edge, [mmA4[-1], mmB4[-1]], P[2], P[4], 512,
                              stvC, rmvC[:, 0:512], mrvC[:, 0:512])
            f1 = _ln_finalize(pr, nc, edge, [mmA4[-1], mmB4[-1], f0], P[3], P[5],
                              FH - 512, stvC, rmvC[:, 512:512 + FH - 512],
                              mrvC[:, 512:512 + FH - 512])
            bc4 = pr.tensor(lambda te: (
                te.matmul(P[2][:, 0:512], ones1x[:], rmvC[:, 0:512], start=True, stop=True),
                te.matmul(P[3][:, 0:FH - 512], ones1x[:], rmvC[:, 512:512 + FH - 512],
                          start=True, stop=True),
                te.matmul(P[4][:, 0:512], ones1x[:], mrvC[:, 0:512], start=True, stop=True),
                te.matmul(P[5][:, 0:FH - 512], ones1x[:], mrvC[:, 512:512 + FH - 512],
                          start=True, stop=True))[-1], deps=[f1])
            xfree = []
            hdmas = []
            for c in range(NE):
                dd = [hdmas[c - 2]] if c >= 2 else [sq4[NE - 2 + c], mmA4[NE - 2 + c]]
                ld = pr.sync(lambda sy, c=c, co=co: sy.dma_start(
                    out=xs[c % 2][:], in_=x1d[128 * c:128 * (c + 1), co:co + FH]),
                    deps=dd + ([mmB4[-1]] if c < 2 else []))
                def res(v, c=c):
                    for ti, (to, tn) in enumerate(TT2):
                        v.tensor_tensor(sqC[c % 2][:, 0:tn],
                                        ffno[:, FH * c + to:FH * c + to + tn],
                                        P[2 + ti][:, 0:tn], ALU.mult)
                        v.tensor_tensor(sqC[c % 2][:, 0:tn], sqC[c % 2][:, 0:tn],
                                        P[4 + ti][:, 0:tn], ALU.subtract)
                        last = v.tensor_tensor(xs[c % 2][:, to:to + tn],
                                               xs[c % 2][:, to:to + tn],
                                               sqC[c % 2][:, 0:tn], ALU.add)
                    return last
                rv = pr.vector(res, deps=[ld, bc4] + ([hdmas[c - 2]] if c >= 2 else []))
                dm = pr.gpsimd(lambda gp, c=c, co=co: gp.dma_start(
                    out=x2d[128 * c:128 * (c + 1), co:co + FH], in_=xs[c % 2][:]),
                    deps=[rv], dma=True)
                hdmas.append(dm)
                xfree.append(rv)
            x2dmas += hdmas
            prev_half = dict(xfree=xfree, hdm=hdmas, fin3=bc4, lastg=w2qs[-1])
        if d_x2 is not None:
            for c in range(NE):
                pr.gpsimd(lambda gp, c=c: gp.dma_start(
                    out=d_x2[128 * c:128 * (c + 1), :],
                    in_=x2d[128 * c:128 * (c + 1), :]), deps=x2dmas[-2:], dma=True)
        pr.barrier()
        pr.compile(nc, ges)


def _phaseD(nc, ges, L):
    g = lambda n: L[n]
    x2d, x3d, d_x3 = g("x2d"), g("x3d"), g("d_x3")
    l1wqs, l1wkvs, l1wk, l1wv, l1wos = g("l1wqs"), g("l1wkvs"), g("l1wk"), g("l1wv"), g("l1wos")
    l1mA, l1mP = g("l1mA"), g("l1mP")
    ones128f, ones1x, ones16f, edge = g("ones128f"), g("ones1x"), g("ones16f"), g("edge")
    ones128b, ones16b = g("ones128b"), g("ones16b")
    TT3 = [(0, 512), (512, 512), (1024, TOK - 1024)]
    TQ = [(0, 512), (512, 512)]
    esD = ExitStack()
    A = lambda n, s, d=F32: esD.enter_context(nc.sbuf_tensor(n, s, d))
    PS = lambda n: esD.enter_context(nc.psum_tensor(n, [128, 512], F32))
    with esD:
        av2 = A("av2", [128, H * 1024], BF16)
        l_all = A("l_all", [16, 1024])
        linvD = A("linvD", [16, 1024])
        levh = A("levh", [1, 1024])
        mAb = A("mAb", [128, 8 * 2 * 128], BF16)
        mPb = A("mPb", [16, 8 * 128], BF16)
        stvD = A("stvD", [1, 4 * 512])
        rmvD = A("rmvD", [1, 3 * 512])
        mrvD = A("mrvD", [1, 3 * 512])
        sqD = [A("sqD0", [128, 512]), A("sqD1", [128, 512])]
        nmxD = A("nmxD", [1, 8])
        nrD = A("nrD", [1, 2048], BF16)
        nrKD = A("nrKD", [1, 1280], BF16)
        negcD = A("negcD", [128, 1])
        lrowD = A("lrowD", [1, 1024])
        P = [PS(f"PD{i}") for i in range(8)]
        esQ = ExitStack()
        AQ = lambda n, s, d=F32: esQ.enter_context(nc.sbuf_tensor(n, s, d))
        q_sb = AQ("q_sb", [128, H * 1024], BF16)
        kvl = AQ("kvl", [128, 4 * 1280], BF16)
        pr = Prog("pD")
        cmap = {}

        mld = pr.sync(lambda sy: sy.dma_start(out=mAb[:], in_=l1mA[:, :]))
        mld2 = pr.sync(lambda sy: sy.dma_start(out=mPb[:], in_=l1mP[:, :]))
        es1 = ExitStack()
        A1 = lambda n, s, d=F32: es1.enter_context(nc.sbuf_tensor(n, s, d))
        xln5 = es1.enter_context(nc.sbuf_tensor("xln5", [128, NE * TOK], BF16))
        xsD = [A1(f"xsD{i}", [128, TOK]) for i in range(2)]
        rbD = A1("rbD", [128, TOK])
        mrbD = A1("mrbD", [128, TOK])
        wstgD = A1("wstgD", [128, 2048])
        wqD = A1("wqD", [128, 2048], BF16)
        lds, sqs, mmAs, mmBs = [], [], [], []
        for c in range(NE):
            dd = [mmAs[c - 2], sqs[c - 2]] if c >= 2 else []
            ld = pr.sync(lambda sy, c=c: sy.dma_start(
                out=xsD[c % 2][:], in_=x2d[128 * c:128 * (c + 1), :]), deps=dd)
            lds.append(ld)
            def mA(te, c=c):
                for ti, (to, tn) in enumerate(TT3):
                    last = te.matmul(P[2 + ti][0:1, 0:tn], ones128f[:],
                                     xsD[c % 2][:, to:to + tn],
                                     start=(c == 0), stop=(c == NE - 1),
                                     skip_group_check=True)
                return last
            mmAs.append(pr.tensor(mA, deps=[ld]))
            sq_ = pr.scalar(lambda s, c=c: s.activation(
                out=sqD[0][:, 0:512] if False else None, in_=None, func=None)
                if False else s.activation(out=xsD[c % 2][:] if False else None,
                                           in_=None, func=None)
                if False else s.activation(out=sqD[0][:], in_=xsD[c % 2][:, 0:512],
                                           func=AF.Square),
                deps=[ld] + ([mmBs[c - 1]] if c >= 1 else []))
            sq2_ = pr.scalar(lambda s, c=c: s.activation(
                out=sqD[1][:], in_=xsD[c % 2][:, 512:1024], func=AF.Square), deps=[sq_])
            sq3_ = pr.scalar(lambda s, c=c: s.activation(
                out=sqD[0][:, 0:TOK - 1024] if False else None, in_=None, func=None)
                if False else s.activation(out=levh[:] if False else None, in_=None,
                                           func=None)
                if False else None, deps=[]) if False else None
            sqs.append(sq2_)
            def mB(te, c=c):
                te.matmul(P[5][0:1, 0:512], ones128f[:], sqD[0][:],
                          start=(c == 0), stop=(c == NE - 1), skip_group_check=True)
                return te.matmul(P[6][0:1, 0:512], ones128f[:], sqD[1][:],
                                 start=(c == 0), stop=(c == NE - 1),
                                 skip_group_check=True)
            mmBs.append(pr.tensor(mB, deps=[sq2_]))
            sqt3 = pr.scalar(lambda s, c=c: s.activation(
                out=sqD[0][:, 0:TOK - 1024], in_=xsD[c % 2][:, 1024:TOK],
                func=AF.Square), deps=[mmBs[c]])
            mB3 = pr.tensor(lambda te, c=c: te.matmul(
                P[7][0:1, 0:TOK - 1024], ones128f[:], sqD[0][:, 0:TOK - 1024],
                start=(c == 0), stop=(c == NE - 1), skip_group_check=True), deps=[sqt3])
            sqs[-1] = sqt3
            mmBs[-1] = mB3
        fins = []
        pBmap = [(P[2], P[5]), (P[3], P[6]), (P[4], P[7])]
        for ti, (to, tn) in enumerate(TT3):
            dd = [mmAs[-1], mmBs[-1]] + ([fins[-1]] if fins else [])
            fins.append(_ln_finalize(pr, nc, edge, dd, pBmap[ti][0], pBmap[ti][1], tn,
                                     stvD, rmvD[:, 512 * ti:512 * ti + tn],
                                     mrvD[:, 512 * ti:512 * ti + tn]))
        bc5 = fins[-1]
        for ti, (to, tn) in enumerate(TT3):
            bc5 = pr.tensor(lambda te, ti=ti, tn=tn: (
                te.matmul(pBmap[ti][0][:, 0:tn], ones1x[:],
                          rmvD[:, 512 * ti:512 * ti + tn], start=True, stop=True),
                te.matmul(pBmap[ti][1][:, 0:tn], ones1x[:],
                          mrvD[:, 512 * ti:512 * ti + tn], start=True, stop=True)
            )[-1], deps=[bc5])
        TTO = [0, 512, 1024]
        ev5 = bc5
        for ti, (to, tn) in enumerate(TT3):
            ev5 = pr.scalar(lambda s, ti=ti, tn=tn: s.copy(
                rbD[:, TTO[ti]:TTO[ti] + tn], pBmap[ti][0][:, 0:tn]), deps=[ev5])
            ev5 = pr.scalar(lambda s, ti=ti, tn=tn: s.copy(
                mrbD[:, TTO[ti]:TTO[ti] + tn], pBmap[ti][1][:, 0:tn]), deps=[ev5])
        xlns = []
        for c in range(NE):
            dd = [xlns[c - 2]] if c >= 2 else [sqs[NE - 2 + c], mmAs[NE - 2 + c]]
            ld = pr.sync(lambda sy, c=c: sy.dma_start(
                out=xsD[c % 2][:], in_=x2d[128 * c:128 * (c + 1), :]),
                deps=dd + ([mmBs[-1]] if c < 2 else []))
            def xl(v, c=c):
                for ti, (to, tn) in enumerate(TT3):
                    v.tensor_tensor(sqD[c % 2][:, 0:tn], xsD[c % 2][:, to:to + tn],
                                    rbD[:, TTO[ti]:TTO[ti] + tn], ALU.mult)
                    last = v.tensor_tensor(xln5[:, TOK * c + to:TOK * c + to + tn],
                                           sqD[c % 2][:, 0:tn],
                                           mrbD[:, TTO[ti]:TTO[ti] + tn],
                                           ALU.subtract)
                return last
            xlns.append(pr.vector(xl, deps=[ld, ev5] + ([xlns[c - 2]] if c >= 2 else [])))
        qevs = []
        wqcasts = []
        for ec in range(NE):
            lw = pr.sync(lambda sy, ec=ec: sy.dma_start(
                out=wstgD[:].rearrange("p (c j) -> p c j", j=128),
                in_=l1wqs[ec].rearrange("(c p) j -> p c j", p=128)),
                deps=([wqcasts[-1], qevs[-1]] if wqcasts else []))
            wc = pr.vector(lambda v: v.tensor_scalar(wqD[:], wstgD[:], SC1, None,
                                                     ALU.mult), deps=[lw])
            wqcasts.append(wc)
            for ti, (to, tn) in enumerate(TQ):
                un = 2 * ec + ti
                dd = [wc, xlns[-1]]
                if un >= 2:
                    dd.append(qevs[un - 2])
                else:
                    dd.append(bc5)
                def qm(te, to=to, tn=tn, un=un):
                    for c in range(NE):
                        last = te.matmul(P[un % 2][:, 0:tn],
                                         wqD[:, 128 * c:128 * (c + 1)],
                                         xln5[:, TOK * c + 144 + to:TOK * c + 144 + to + tn],
                                         start=(c == 0), stop=(c == NE - 1))
                    return last
                qmm = pr.tensor(qm, deps=dd)
                qe = pr.scalar(lambda s, ec=ec, to=to, tn=tn, un=un: s.copy(
                    q_sb[:, 1024 * ec + to:1024 * ec + to + tn], P[un % 2][:, 0:tn]),
                    deps=[qmm])
                qevs.append(qe)
        KPC = [(0, 0, 16), (128, 16, 512), (640, 528, 512), (1152, 1040, 128)]
        kvevs = []
        for rc in range(4):
            lw = pr.sync(lambda sy, rc=rc: sy.dma_start(
                out=wstgD[:].rearrange("p (c j) -> p c j", j=128),
                in_=l1wkvs[rc].rearrange("(c p) j -> p c j", p=128)),
                deps=[wqcasts[-1]] if rc == 0 else [kvcast])
            kvcast = pr.vector(lambda v: v.tensor_copy(wqD[:], wstgD[:]),
                               deps=[lw, qevs[-1]] if rc == 0 else [lw, kvevs[-1]])
            for pi, (ko, xo, n) in enumerate(KPC):
                un = 4 * rc + pi
                dd = [kvcast, xlns[-1]]
                if len(kvevs) >= 2:
                    dd.append(kvevs[-2])
                else:
                    dd.append(qevs[-(2 - len(kvevs))])
                def km(te, ko=ko, xo=xo, n=n, un=un):
                    for c in range(NE):
                        last = te.matmul(P[un % 2][:, 0:n],
                                         wqD[:, 128 * c:128 * (c + 1)],
                                         xln5[:, TOK * c + xo:TOK * c + xo + n],
                                         start=(c == 0), stop=(c == NE - 1))
                    return last
                kmm = pr.tensor(km, deps=dd)
                ke = pr.scalar(lambda s, rc=rc, ko=ko, n=n, un=un: s.copy(
                    kvl[:, 1280 * rc + ko:1280 * rc + ko + n], P[un % 2][:, 0:n]),
                    deps=[kmm])
                kvevs.append(ke)
        zpad = pr.vector(lambda v: [v.memset(kvl[:, 1280 * rc + 16:1280 * rc + 128],
                                             0.0) for rc in range(4)][-1],
                         deps=kvevs[-2:])
        es1.close()

        es2 = ExitStack()
        A2 = lambda n, s, d=F32: es2.enter_context(nc.sbuf_tensor(n, s, d))
        khb = A2("khb", [128, 1280], BF16)
        vTb = A2("vTb", [128, 10 * 128], BF16)
        wkvstg = A2("wkvstg", [128, 8 * 128])
        wkvh = A2("wkvh", [128, 8 * 128], BF16)
        exD = [A2(f"exD{i}", [128, 384], BF16) for i in range(2)]
        hstate = dict(psfree={}, lastav=None, lastk=None)
        for h in range(H):
            dd = [hstate["lastk"]] if hstate["lastk"] else [zpad]
            lk = pr.sync(lambda sy, h=h: sy.dma_start(
                out=wkvstg[:, 0:512].rearrange("p (c j) -> p c j", j=128),
                in_=l1wk[:, 128 * h:128 * (h + 1)].rearrange("(c p) j -> p c j", p=128)),
                deps=dd)
            lv = pr.sync(lambda sy, h=h: sy.dma_start(
                out=wkvstg[:, 512:1024].rearrange("p (c j) -> p c j", j=128),
                in_=l1wv[:, 128 * h:128 * (h + 1)].rearrange("(c p) j -> p c j", p=128)),
                deps=dd)
            wch = pr.vector(lambda v: v.tensor_copy(wkvh[:], wkvstg[:]), deps=[lk, lv])
            kps = []
            for pi, (ko, xo, n) in enumerate(KPC):
                dd = [wch, zpad]
                if len(kps) >= 2:
                    dd.append(kps[-2])
                elif hstate["psfree"].get(pi % 2):
                    dd.append(hstate["psfree"][pi % 2])
                def kg(te, ko=ko, n=n, pi=pi):
                    for rc in range(4):
                        last = te.matmul(P[pi % 2][:, 0:n],
                                         wkvh[:, 128 * rc:128 * (rc + 1)],
                                         kvl[:, 1280 * rc + ko:1280 * rc + ko + n],
                                         start=(rc == 0), stop=(rc == 3))
                    return last
                kgm = pr.tensor(kg, deps=dd)
                kge = pr.scalar(lambda s, ko=ko, n=n, pi=pi: s.copy(
                    khb[:, ko:ko + n], P[pi % 2][:, 0:n]), deps=[kgm])
                kps.append(kge)
            vps = []
            for tl in range(10):
                dd = [wch, zpad]
                if len(vps) >= 2:
                    dd.append(vps[-2])
                else:
                    dd.append(kps[-(2 - len(vps))])
                def vg(te, tl=tl):
                    for rc in range(4):
                        last = te.matmul(P[tl % 2][0:128, 0:128],
                                         kvl[:, 1280 * rc + 128 * tl:1280 * rc + 128 * (tl + 1)],
                                         wkvh[:, 512 + 128 * rc:512 + 128 * (rc + 1)],
                                         start=(rc == 0), stop=(rc == 3))
                    return last
                vgm = pr.tensor(vg, deps=dd)
                vge = pr.scalar(lambda s, tl=tl: s.copy(
                    vTb[:, 128 * tl:128 * (tl + 1)], P[tl % 2][0:128, 0:128]), deps=[vgm])
                vps.append(vge)
            nsq1 = pr.scalar(lambda s, h=h: s.activation(
                out=sqD[0][:], in_=q_sb[:, 1024 * h:1024 * h + 512], func=AF.Square),
                deps=[qevs[2 * h], vps[-1]])
            nmm1 = pr.tensor(lambda te: te.matmul(P[2][0:1, 0:512], ones128f[:],
                                                  sqD[0][:], start=True, stop=True),
                             deps=[nsq1, kps[-1]])
            nsq2 = pr.scalar(lambda s, h=h: s.activation(
                out=sqD[1][:], in_=q_sb[:, 1024 * h + 512:1024 * (h + 1)],
                func=AF.Square), deps=[qevs[2 * h + 1]])
            nmm2 = pr.tensor(lambda te: te.matmul(P[3][0:1, 0:512], ones128f[:],
                                                  sqD[1][:], start=True, stop=True),
                             deps=[nsq2])
            nq = pr.vector(lambda v: (
                v.tensor_copy(nrD[0:1, 0:512], P[2][0:1, 0:512]),
                v.tensor_copy(nrD[0:1, 512:1024], P[3][0:1, 0:512]))[-1],
                deps=[nmm1, nmm2])
            kn = []
            for kt in range(3):
                w = [512, 512, 256][kt]
                o = 512 * kt
                ks_ = pr.scalar(lambda s, o=o, w=w, kt=kt: s.activation(
                    out=sqD[kt % 2][:, 0:w], in_=khb[:, o:o + w], func=AF.Square),
                    deps=[kps[-1]] + ([kn[-1]] if kn else [nq]))
                km_ = pr.tensor(lambda te, o=o, w=w, kt=kt: te.matmul(
                    P[2 + kt % 2][0:1, 0:w], ones128f[:], sqD[kt % 2][:, 0:w],
                    start=True, stop=True), deps=[ks_] + ([nq] if kt < 2 else [kn[-1]]))
                ke_ = pr.vector(lambda v, o=o, w=w, kt=kt: v.tensor_copy(
                    nrKD[0:1, o:o + w], P[2 + kt % 2][0:1, 0:w]), deps=[km_])
                kn.append(ke_)
            def vmxD(v):
                v.reduce_max(nmxD[:, 0:1], nrD[0:1, 0:1024], AX)
                edge(v, v.reduce_max(nmxD[:, 1:2], nrKD[0:1, 0:1280], AX))
                return v.tensor_scalar(nmxD[:, 3:4], nmxD[:, 0:1], nmxD[0:1, 1:2],
                                       None, ALU.mult)
            vmx = pr.vector(vmxD, deps=[nq] + kn)
            ssx = pr.scalar(lambda s: s.activation(out=nmxD[:, 4:5], in_=nmxD[:, 3:4],
                                                   func=AF.Sqrt), deps=[vmx])
            cbm = pr.tensor(lambda te: te.matmul(P[2][:, 0:1], ones1x[:], nmxD[:, 4:5],
                                                 start=True, stop=True),
                            deps=[ssx] + ([kn[-1]] if kn else []))
            vnc = pr.vector(lambda v: v.tensor_scalar(negcD[:], P[2][:, 0:1], -1.0,
                                                      None, ALU.mult), deps=[cbm])
            exprev = hstate.get("exprev", [None, None])
            scprev = hstate.get("scprev", [None, None])
            for qt in range(8):
                si = qt % 2
                dd = [kps[-1], vnc, mld, mld2]
                if scprev[si] is not None:
                    dd.append(scprev[si])
                def sc(te, h=h, qt=qt, si=si):
                    te.matmul(P[4 + si][:, 0:128], khb[:, 128 * (1 + qt):128 * (2 + qt)],
                              q_sb[:, 1024 * h + 128 * qt:1024 * h + 128 * (qt + 1)],
                              start=True, stop=True, skip_group_check=True)
                    te.matmul(P[4 + si][:, 128:256], khb[:, 128 * (2 + qt):128 * (3 + qt)],
                              q_sb[:, 1024 * h + 128 * qt:1024 * h + 128 * (qt + 1)],
                              start=True, stop=True, skip_group_check=True)
                    return te.matmul(P[4 + si][0:16, 256:384], khb[:, 0:16],
                                     q_sb[:, 1024 * h + 128 * qt:1024 * h + 128 * (qt + 1)],
                                     start=True, stop=True, skip_group_check=True)
                scm = pr.tensor(sc, deps=dd)
                exdep = [scm]
                if exprev[si] is not None:
                    exdep.append(exprev[si])
                exq = pr.scalar(lambda s, si=si: s.activation(
                    out=exD[si][:], in_=P[4 + si][:, 0:384], func=AF.Exp,
                    bias=negcD[:, 0:1]), deps=exdep)
                msk = pr.vector(lambda v, qt=qt, si=si: (
                    v.tensor_tensor(exD[si][:, 0:128], exD[si][:, 0:128],
                                    mAb[:, 256 * qt:256 * qt + 128], ALU.mult),
                    v.tensor_tensor(exD[si][:, 128:256], exD[si][:, 128:256],
                                    mAb[:, 256 * qt + 128:256 * (qt + 1)], ALU.mult),
                    v.tensor_tensor(exD[si][0:16, 256:384], exD[si][0:16, 256:384],
                                    mPb[:, 128 * qt:128 * (qt + 1)], ALU.mult))[-1],
                    deps=[exq])
                scprev[si] = msk
                ldep = [msk]
                if hstate.get("lastlev") is not None and qt == 0:
                    ldep.append(hstate["lastlev"])
                def lm(te, qt=qt, si=si):
                    te.matmul(P[6][0:1, 0:128], ones128b[:], exD[si][:, 0:128],
                              start=True, stop=False, skip_group_check=True)
                    te.matmul(P[6][0:1, 0:128], ones128b[:], exD[si][:, 128:256],
                              start=False, stop=False, skip_group_check=True)
                    return te.matmul(P[6][0:1, 0:128], ones16b[:], exD[si][0:16, 256:384],
                                     start=False, stop=True, skip_group_check=True)
                lmm = pr.tensor(lm, deps=ldep + ([hstate.get("levcp")] if qt > 0 or hstate.get("levcp") else []))
                lev_ = pr.vector(lambda v, qt=qt: v.tensor_copy(
                    levh[:, 128 * qt:128 * (qt + 1)], P[6][0:1, 0:128]), deps=[lmm])
                hstate["levcp"] = lev_
                avdep = [msk, vps[-1]]
                if hstate["lastav"] is not None and qt == 0:
                    avdep.append(hstate["lastav"])
                def av(te, qt=qt, si=si):
                    te.matmul(P[7][:, 0:128], vTb[:, 128 * (1 + qt):128 * (2 + qt)],
                              exD[si][:, 0:128], start=True, stop=False,
                              skip_group_check=True)
                    te.matmul(P[7][:, 0:128], vTb[:, 128 * (2 + qt):128 * (3 + qt)],
                              exD[si][:, 128:256], start=False, stop=False,
                              skip_group_check=True)
                    return te.matmul(P[7][:, 0:128], vTb[0:16, 0:128],
                                     exD[si][0:16, 256:384], start=False, stop=True,
                                     skip_group_check=True)
                avm = pr.tensor(av, deps=avdep + ([hstate.get("avcp")] if hstate.get("avcp") else []))
                exprev[si] = avm
                ave = pr.scalar(lambda s, h=h, qt=qt: s.copy(
                    av2[:, 1024 * h + 128 * qt:1024 * h + 128 * (qt + 1)], P[7][:, 0:128]),
                    deps=[avm])
                hstate["avcp"] = ave
                hstate["lastav"] = ave
            lcp = pr.gpsimd(lambda gp, h=h: gp.dma_start(
                out=l_all[h:h + 1, :], in_=levh[:, :]), deps=[hstate["levcp"]], dma=True)
            hstate["lastlev"] = lcp
            hstate["exprev"] = exprev
            hstate["scprev"] = scprev
            hstate["psfree"] = {0: scprev[0], 1: scprev[1]}
            hstate["lastk"] = kps[-1]
        vres = pr.vector(lambda v: v.reciprocal(linvD[:], l_all[:]),
                         deps=[hstate["lastlev"]])
        nsteps = []
        lastbcm = None
        for h in range(H):
            dd = [vres]
            if lastbcm is not None:
                dd.append(lastbcm)
            lrdma = pr.gpsimd(lambda gp, h=h: gp.dma_start(
                out=lrowD[:], in_=linvD[h:h + 1, :]), deps=dd, dma=True)
            for ti, (to, tn) in enumerate(TQ):
                un = 2 * h + ti
                dd = [lrdma]
                if len(nsteps) >= 2:
                    dd.append(nsteps[-2])
                else:
                    dd.append(hstate["avcp"])
                bcm = pr.tensor(lambda te, to=to, tn=tn, un=un: te.matmul(
                    P[un % 2][:, 0:tn], ones1x[:], lrowD[0:1, to:to + tn],
                    start=True, stop=True), deps=dd + [hstate["scprev"][un % 2]] if un < 2 else dd)
                lastbcm = bcm
                ntt = pr.vector(lambda v, h=h, to=to, tn=tn, un=un: v.tensor_tensor(
                    av2[:, 1024 * h + to:1024 * h + to + tn],
                    av2[:, 1024 * h + to:1024 * h + to + tn],
                    P[un % 2][:, 0:tn], ALU.mult), deps=[bcm])
                nsteps.append(ntt)
        es2.close()
        esQ.close()
        es3 = ExitStack()
        A3 = lambda n, s, d=F32: es3.enter_context(nc.sbuf_tensor(n, s, d))
        aoD = A3("aoD", [128, NE * 1024])
        wstg3 = A3("wstg3", [128, 2048])
        wob = A3("wob", [128, 2048], BF16)
        xsD3 = [A3(f"xsD3{i}", [128, 1024]) for i in range(2)]
        x3t = [A3(f"x3t{i}", [128, 1024]) for i in range(2)]
        evs, stm = {}, {}
        wcs = []
        for ec in range(NE):
            dd = [wcs[-1]] if wcs else nsteps[-2:]
            lw = pr.sync(lambda sy, ec=ec: sy.dma_start(
                out=wstg3[:].rearrange("p (c j) -> p c j", j=128),
                in_=l1wos[ec].rearrange("(c p) j -> p c j", p=128)),
                deps=dd + ([evs[2 * ec - 1]] if ec >= 1 else []))
            wc = pr.vector(lambda v: v.tensor_copy(wob[:], wstg3[:]), deps=[lw])
            wcs.append(wc)
            for ti, (to, tn) in enumerate(TQ):
                un = 2 * ec + ti
                dd = [wc] + nsteps
                if un >= 2:
                    dd.append(evs[un - 2])
                def om(te, to=to, tn=tn, un=un):
                    for fc in range(NE):
                        last = te.matmul(P[2 + un % 2][:, 0:tn],
                                         wob[:, 128 * fc:128 * (fc + 1)],
                                         av2[:, 1024 * fc + to:1024 * fc + to + tn],
                                         start=(fc == 0), stop=(fc == NE - 1))
                    return last
                omm = pr.tensor(om, deps=dd)
                ev = pr.scalar(lambda s, ec=ec, to=to, tn=tn, un=un: s.copy(
                    aoD[:, 1024 * ec + to:1024 * ec + to + tn], P[2 + un % 2][:, 0:tn]),
                    deps=[omm])
                evs[un] = ev
                sq_ = pr.scalar(lambda s, ec=ec, to=to, tn=tn, un=un: s.activation(
                    out=sqD[un % 2][:], in_=aoD[:, 1024 * ec + to:1024 * ec + to + tn],
                    func=AF.Square), deps=[ev] + ([stm[(un - 2, 1)]] if (un - 2, 1) in stm else []))
                m1 = pr.tensor(lambda te, ti=ti, to=to, tn=tn, ec=ec: te.matmul(
                    P[4 + ti][0:1, 0:tn], ones128f[:],
                    aoD[:, 1024 * ec + to:1024 * ec + to + tn],
                    start=(ec == 0), stop=(ec == NE - 1), skip_group_check=True),
                    deps=[ev])
                m2 = pr.tensor(lambda te, ti=ti, tn=tn, un=un: te.matmul(
                    P[6 + ti][0:1, 0:tn], ones128f[:], sqD[un % 2][:],
                    start=(un // 2 == 0), stop=(un // 2 == NE - 1),
                    skip_group_check=True), deps=[sq_])
                stm[(un, 0)] = m1
                stm[(un, 1)] = m2
        finsD = []
        for ti, (to, tn) in enumerate(TQ):
            dd = [stm[(2 * (NE - 1) + ti, 0)], stm[(2 * (NE - 1) + ti, 1)]]
            if finsD:
                dd.append(finsD[-1])
            finsD.append(_ln_finalize(pr, nc, edge, dd, P[4 + ti], P[6 + ti], tn,
                                      stvD, rmvD[:, 512 * ti:512 * ti + tn],
                                      mrvD[:, 512 * ti:512 * ti + tn]))
        bc6 = pr.tensor(lambda te: (
            te.matmul(P[4][:, 0:512], ones1x[:], rmvD[:, 0:512], start=True, stop=True),
            te.matmul(P[5][:, 0:512], ones1x[:], rmvD[:, 512:1024], start=True, stop=True),
            te.matmul(P[6][:, 0:512], ones1x[:], mrvD[:, 0:512], start=True, stop=True),
            te.matmul(P[7][:, 0:512], ones1x[:], mrvD[:, 512:1024], start=True, stop=True)
        )[-1], deps=[finsD[-1]])
        x3dmas = []
        for c in range(NE):
            dd = [x3dmas[c - 2]] if c >= 2 else []
            lx = pr.sync(lambda sy, c=c: sy.dma_start(
                out=xsD3[c % 2][:], in_=x2d[128 * c:128 * (c + 1), 144:TOK]), deps=dd)
            def res(v, c=c):
                for ti, (to, tn) in enumerate(TQ):
                    v.tensor_tensor(x3t[c % 2][:, to:to + tn],
                                    aoD[:, 1024 * c + to:1024 * c + to + tn],
                                    P[4 + ti][:, 0:tn], ALU.mult)
                    v.tensor_tensor(x3t[c % 2][:, to:to + tn], x3t[c % 2][:, to:to + tn],
                                    P[6 + ti][:, 0:tn], ALU.subtract)
                    last = v.tensor_tensor(x3t[c % 2][:, to:to + tn],
                                           x3t[c % 2][:, to:to + tn],
                                           xsD3[c % 2][:, to:to + tn], ALU.add)
                return last
            rv = pr.vector(res, deps=[lx, bc6, evs[2 * c], evs[2 * c + 1]] +
                           ([x3dmas[c - 2]] if c >= 2 else []))
            dm = pr.gpsimd(lambda gp, c=c: gp.dma_start(
                out=x3d[128 * c:128 * (c + 1), :], in_=x3t[c % 2][:]), deps=[rv],
                dma=True)
            x3dmas.append(dm)
        if d_x3 is not None:
            for c in range(NE):
                pr.gpsimd(lambda gp, c=c: gp.dma_start(
                    out=d_x3[128 * c:128 * (c + 1), :],
                    in_=x3d[128 * c:128 * (c + 1), :]), deps=x3dmas[-2:], dma=True)
        es3.close()
        pr.barrier()
        pr.compile(nc, ges)


def _phaseE(nc, ges, L):
    g = lambda n: L[n]
    x3d, y = g("x3d"), g("y")
    g1si, u1si, d1si = g("g1si"), g("u1si"), g("d1si")
    sgp, sup, sdp = g("sgp"), g("sup"), g("sdp")
    ones128f, ones1x, edge = g("ones128f"), g("ones1x"), g("edge")
    ones128b = g("ones128b")
    TQ = [(0, 512), (512, 512)]
    esE = ExitStack()
    A = lambda n, s, d=F32: esE.enter_context(nc.sbuf_tensor(n, s, d))
    PS = lambda n: esE.enter_context(nc.psum_tensor(n, [128, 512], F32))
    with esE:
        hid2 = A("hid2", [128, NH2 * 1024], BF16)
        sgs = A("sgs", [128, NH2]); sus = A("sus", [128, NH2]); sds = A("sds", [128, NE])
        s1n = A("s1n", [1, 1024]); s2n = A("s2n", [1, 1024])
        xsE = [A(f"xsE{i}", [128, 1024]) for i in range(2)]
        sqE = [A(f"sqE{i}", [128, 512]) for i in range(2)]
        stvE = A("stvE", [1, 4 * 512])
        rmvE = A("rmvE", [1, 2 * 512])
        mrvE = A("mrvE", [1, 2 * 512])
        P = [PS(f"PE{i}") for i in range(8)]
        pr = Prog("pE")
        sld0 = pr.sync(lambda sy: sy.dma_start(out=sgs[:], in_=sgp[:, :]))
        sld1 = pr.sync(lambda sy: sy.dma_start(out=sus[:], in_=sup[:, :]),
                       deps=[sld0])
        sld = pr.sync(lambda sy: sy.dma_start(out=sds[:], in_=sdp[:, :]),
                      deps=[sld1])
        es1 = ExitStack()
        A1 = lambda n, s, d=F32: es1.enter_context(nc.sbuf_tensor(n, s, d))
        xln7 = A1("xln7", [128, NE * 1024], BF16)
        gbb = [A1(f"gbb{i}", [128, 2048], BF16) for i in range(2)]
        ubb = [A1(f"ubb{i}", [128, 2048], BF16) for i in range(2)]
        gsc = A1("gsc", [128, 1024], BF16)
        lds, sqs, mmAs, mmBs = [], [], [], []
        for c in range(NE):
            dd = [mmAs[c - 2], sqs[c - 2]] if c >= 2 else []
            ld = pr.sync(lambda sy, c=c: sy.dma_start(
                out=xsE[c % 2][:], in_=x3d[128 * c:128 * (c + 1), :]), deps=dd)
            lds.append(ld)
            def mA(te, c=c):
                for ti, (to, tn) in enumerate(TQ):
                    last = te.matmul(P[2 + ti][0:1, 0:tn], ones128f[:],
                                     xsE[c % 2][:, to:to + tn],
                                     start=(c == 0), stop=(c == NE - 1),
                                     skip_group_check=True)
                return last
            mmAs.append(pr.tensor(mA, deps=[ld]))
            sq1 = pr.scalar(lambda s, c=c: s.activation(
                out=sqE[0][:], in_=xsE[c % 2][:, 0:512], func=AF.Square),
                deps=[ld] + ([mmBs[c - 1]] if c >= 1 else []))
            sq2 = pr.scalar(lambda s, c=c: s.activation(
                out=sqE[1][:], in_=xsE[c % 2][:, 512:1024], func=AF.Square), deps=[sq1])
            sqs.append(sq2)
            def mB(te, c=c):
                te.matmul(P[4][0:1, 0:512], ones128f[:], sqE[0][:],
                          start=(c == 0), stop=(c == NE - 1), skip_group_check=True)
                return te.matmul(P[5][0:1, 0:512], ones128f[:], sqE[1][:],
                                 start=(c == 0), stop=(c == NE - 1),
                                 skip_group_check=True)
            mmBs.append(pr.tensor(mB, deps=[sq2]))
        f0 = _ln_finalize(pr, nc, edge, [mmAs[-1], mmBs[-1]], P[2], P[4], 512,
                          stvE, rmvE[:, 0:512], mrvE[:, 0:512])
        f1 = _ln_finalize(pr, nc, edge, [mmAs[-1], mmBs[-1], f0], P[3], P[5], 512,
                          stvE, rmvE[:, 512:1024], mrvE[:, 512:1024])
        bc7 = pr.tensor(lambda te: (
            te.matmul(P[2][:, 0:512], ones1x[:], rmvE[:, 0:512], start=True, stop=True),
            te.matmul(P[3][:, 0:512], ones1x[:], rmvE[:, 512:1024], start=True, stop=True),
            te.matmul(P[4][:, 0:512], ones1x[:], mrvE[:, 0:512], start=True, stop=True),
            te.matmul(P[5][:, 0:512], ones1x[:], mrvE[:, 512:1024], start=True, stop=True)
        )[-1], deps=[f1])
        xlns = []
        for c in range(NE):
            dd = [xlns[c - 2]] if c >= 2 else [sqs[NE - 2 + c], mmAs[NE - 2 + c]]
            ld = pr.sync(lambda sy, c=c: sy.dma_start(
                out=xsE[c % 2][:], in_=x3d[128 * c:128 * (c + 1), :]),
                deps=dd + ([mmBs[-1]] if c < 2 else []))
            def xl(v, c=c):
                for ti, (to, tn) in enumerate(TQ):
                    v.tensor_tensor(sqE[c % 2][:], xsE[c % 2][:, to:to + tn],
                                    P[2 + ti][:, 0:512], ALU.mult)
                    last = v.tensor_tensor(xln7[:, 1024 * c + to:1024 * c + to + tn],
                                           sqE[c % 2][:], P[4 + ti][:, 0:512],
                                           ALU.subtract)
                return last
            xlns.append(pr.vector(xl, deps=[ld, bc7] + ([xlns[c - 2]] if c >= 2 else [])))
        def sig1(te):
            for c in range(NE):
                for ti, (to, tn) in enumerate(TQ):
                    last = te.matmul(P[6 + ti][0:1, 0:tn], ones128b[:],
                                     xln7[:, 1024 * c + to:1024 * c + to + tn],
                                     start=(c == 0), stop=(c == NE - 1),
                                     skip_group_check=True)
            return last
        s1mm = pr.tensor(sig1, deps=[xlns[-1]])
        s1ev = pr.vector(lambda v: (
            v.tensor_scalar(s1n[:, 0:512], P[6][0:1, 0:512], -128.0, None, ALU.mult),
            v.tensor_scalar(s1n[:, 512:1024], P[7][0:1, 0:512], -128.0, None, ALU.mult)
        )[-1], deps=[s1mm])
        prodsteps = []
        s2mms = []
        gmms, umms = [], []
        for hc in range(NH2):
            ldd = [gmms[hc - 2]] if hc >= 2 else []
            lg_ = pr.sync(lambda sy, hc=hc: sy.dma_start(
                out=gbb[hc % 2][:].rearrange("p (c j) -> p c j", j=128),
                in_=g1si[hc].rearrange("(c p) j -> p c j", p=128)), deps=ldd)
            def gmm(te, hc=hc):
                for ti, (to, tn) in enumerate(TQ):
                    for c in range(NE):
                        te.matmul(P[ti][:, 0:tn], gbb[hc % 2][:, 128 * c:128 * (c + 1)],
                                  xln7[:, 1024 * c + to:1024 * c + to + tn],
                                  start=(c == 0), stop=False, skip_group_check=True)
                    last = te.matmul(P[ti][:, 0:tn], ones1x[:], s1n[:, to:to + tn],
                                     start=False, stop=True, skip_group_check=True)
                return last
            gm = pr.tensor(gmm, deps=[lg_, s1ev, xlns[-1]] +
                           ([prodsteps[hc - 1]] if hc >= 1 else [bc7]))
            gmms.append(gm)
            gev = pr.scalar(lambda s, hc=hc: (
                s.activation(out=gsc[:, 0:512], in_=P[0][:, 0:512], func=AF.Silu,
                             scale=sgs[:, hc:hc + 1]),
                s.activation(out=gsc[:, 512:1024], in_=P[1][:, 0:512], func=AF.Silu,
                             scale=sgs[:, hc:hc + 1]))[-1], deps=[gm, sld])
            lu_ = pr.sync(lambda sy, hc=hc: sy.dma_start(
                out=ubb[hc % 2][:].rearrange("p (c j) -> p c j", j=128),
                in_=u1si[hc].rearrange("(c p) j -> p c j", p=128)),
                deps=[umms[hc - 2]] if hc >= 2 else [])
            def umm(te, hc=hc):
                for ti, (to, tn) in enumerate(TQ):
                    for c in range(NE):
                        te.matmul(P[ti][:, 0:tn], ubb[hc % 2][:, 128 * c:128 * (c + 1)],
                                  xln7[:, 1024 * c + to:1024 * c + to + tn],
                                  start=(c == 0), stop=False, skip_group_check=True)
                    last = te.matmul(P[ti][:, 0:tn], ones1x[:], s1n[:, to:to + tn],
                                     start=False, stop=True, skip_group_check=True)
                return last
            um = pr.tensor(umm, deps=[lu_, gev])
            umms.append(um)
            def prod(v, hc=hc):
                v.tensor_scalar(sqE[0][:], P[0][:, 0:512], sus[:, hc:hc + 1],
                                None, ALU.mult)
                v.tensor_scalar(sqE[1][:], P[1][:, 0:512], sus[:, hc:hc + 1],
                                None, ALU.mult)
                v.tensor_tensor(hid2[:, 1024 * hc:1024 * hc + 512], gsc[:, 0:512],
                                sqE[0][:], ALU.mult)
                return v.tensor_tensor(hid2[:, 1024 * hc + 512:1024 * (hc + 1)],
                                       gsc[:, 512:1024], sqE[1][:], ALU.mult)
            pd = pr.vector(prod, deps=[um, gev, sld])
            prodsteps.append(pd)
            def s2m(te, hc=hc):
                te.matmul(P[6][0:1, 0:512], ones128b[:], hid2[:, 1024 * hc:1024 * hc + 512],
                          start=(hc == 0), stop=(hc == NH2 - 1), skip_group_check=True)
                return te.matmul(P[7][0:1, 0:512], ones128b[:],
                                 hid2[:, 1024 * hc + 512:1024 * (hc + 1)],
                                 start=(hc == 0), stop=(hc == NH2 - 1),
                                 skip_group_check=True)
            s2mms.append(pr.tensor(s2m, deps=[pd] + ([s1mm] if hc == 0 else [])))
        s2ev = pr.vector(lambda v: (
            v.tensor_scalar(s2n[:, 0:512], P[6][0:1, 0:512], -128.0, None, ALU.mult),
            v.tensor_scalar(s2n[:, 512:1024], P[7][0:1, 0:512], -128.0, None, ALU.mult)
        )[-1], deps=[s2mms[-1]])
        es1.close()
        es2 = ExitStack()
        A2 = lambda n, s, d=F32: es2.enter_context(nc.sbuf_tensor(n, s, d))
        dbqd = [A2(f"dbq{i}", [128, NH2 * 128], BF16) for i in range(2)]
        ffnoE = A2("ffnoE", [128, NE * 1024], BF16)
        yt = [A2(f"yt{i}", [128, 1024]) for i in range(2)]
        evE, stmE = {}, {}
        dstate = {}
        for ec in range(NE):
            ldd = [dstate[f"m{ec-2}"]] if f"m{ec-2}" in dstate else [s2ev]
            ldn = pr.sync(lambda sy, ec=ec: sy.dma_start(
                out=dbqd[ec % 2][:].rearrange("p (c j) -> p c j", j=128),
                in_=d1si[ec].rearrange("(c p) j -> p c j", p=128)), deps=ldd)
            def dmm(te, ec=ec):
                for ti, (to, tn) in enumerate(TQ):
                    for hc in range(NH2):
                        te.matmul(P[2 + ti][:, 0:tn],
                                  dbqd[ec % 2][:, 128 * hc:128 * (hc + 1)],
                                  hid2[:, 1024 * hc + to:1024 * hc + to + tn],
                                  start=(hc == 0), stop=False, skip_group_check=True)
                    last = te.matmul(P[2 + ti][:, 0:tn], ones1x[:], s2n[:, to:to + tn],
                                     start=False, stop=True, skip_group_check=True)
                return last
            dm = pr.tensor(dmm, deps=[ldn, s2ev] +
                           ([evE[(ec - 1, 1)]] if (ec - 1, 1) in evE else []))
            dstate[f"m{ec}"] = dm
            ev = pr.vector(lambda v, ec=ec: (
                v.tensor_scalar(ffnoE[:, 1024 * ec:1024 * ec + 512], P[2][:, 0:512],
                                sds[:, ec:ec + 1], None, ALU.mult),
                v.tensor_scalar(ffnoE[:, 1024 * ec + 512:1024 * (ec + 1)], P[3][:, 0:512],
                                sds[:, ec:ec + 1], None, ALU.mult))[-1], deps=[dm, sld])
            evE[(ec, 1)] = ev
            sq_ = pr.scalar(lambda s, ec=ec: (
                s.activation(out=sqE[0][:], in_=ffnoE[:, 1024 * ec:1024 * ec + 512],
                             func=AF.Square),
                s.activation(out=sqE[1][:], in_=ffnoE[:, 1024 * ec + 512:1024 * (ec + 1)],
                             func=AF.Square))[-1],
                deps=[ev] + ([stmE[(ec - 1, 1)]] if (ec - 1, 1) in stmE else []))
            m1 = pr.tensor(lambda te, ec=ec: (
                te.matmul(P[4][0:1, 0:512], ones128b[:], ffnoE[:, 1024 * ec:1024 * ec + 512],
                          start=(ec == 0), stop=(ec == NE - 1), skip_group_check=True),
                te.matmul(P[5][0:1, 0:512], ones128b[:],
                          ffnoE[:, 1024 * ec + 512:1024 * (ec + 1)],
                          start=(ec == 0), stop=(ec == NE - 1), skip_group_check=True)
            )[-1], deps=[ev])
            m2 = pr.tensor(lambda te, ec=ec: (
                te.matmul(P[6][0:1, 0:512], ones128f[:], sqE[0][:],
                          start=(ec == 0), stop=(ec == NE - 1), skip_group_check=True),
                te.matmul(P[7][0:1, 0:512], ones128f[:], sqE[1][:],
                          start=(ec == 0), stop=(ec == NE - 1), skip_group_check=True)
            )[-1], deps=[sq_])
            stmE[(ec, 0)] = m1
            stmE[(ec, 1)] = m2
        f0 = _ln_finalize(pr, nc, edge, [stmE[(NE - 1, 0)], stmE[(NE - 1, 1)]],
                          P[4][0:1, 0:512] if False else P[4], P[6], 512, stvE,
                          rmvE[:, 0:512], mrvE[:, 0:512])
        f1 = _ln_finalize(pr, nc, edge, [stmE[(NE - 1, 0)], stmE[(NE - 1, 1)], f0],
                          P[5], P[7], 512, stvE, rmvE[:, 512:1024], mrvE[:, 512:1024])
        bc8 = pr.tensor(lambda te: (
            te.matmul(P[4][:, 0:512], ones1x[:], rmvE[:, 0:512], start=True, stop=True),
            te.matmul(P[5][:, 0:512], ones1x[:], rmvE[:, 512:1024], start=True, stop=True),
            te.matmul(P[6][:, 0:512], ones1x[:], mrvE[:, 0:512], start=True, stop=True),
            te.matmul(P[7][:, 0:512], ones1x[:], mrvE[:, 512:1024], start=True, stop=True)
        )[-1], deps=[f1])
        ydmas = []
        for c in range(NE):
            dd = [ydmas[c - 2]] if c >= 2 else []
            lx = pr.sync(lambda sy, c=c: sy.dma_start(
                out=xsE[c % 2][:], in_=x3d[128 * c:128 * (c + 1), :]),
                deps=dd + ([stmE[(NE - 1, 1)], bc8] if c < 2 else []))
            def res(v, c=c):
                for ti, (to, tn) in enumerate(TQ):
                    v.tensor_tensor(yt[c % 2][:, to:to + tn],
                                    ffnoE[:, 1024 * c + to:1024 * c + to + tn],
                                    P[4 + ti][:, 0:512], ALU.mult)
                    v.tensor_tensor(yt[c % 2][:, to:to + tn], yt[c % 2][:, to:to + tn],
                                    P[6 + ti][:, 0:512], ALU.subtract)
                    last = v.tensor_tensor(yt[c % 2][:, to:to + tn],
                                           yt[c % 2][:, to:to + tn],
                                           xsE[c % 2][:, to:to + tn], ALU.add)
                return last
            rv = pr.vector(res, deps=[lx, bc8, evE[(c, 1)]] +
                           ([ydmas[c - 2]] if c >= 2 else []))
            dm = pr.gpsimd(lambda gp, c=c: gp.dma_start(
                out=y[128 * c:128 * (c + 1), :], in_=yt[c % 2][:]), deps=[rv], dma=True)
            ydmas.append(dm)
        es2.close()
        pr.barrier()
        pr.compile(nc, ges)


_CACHE = {}


def kernel(**inputs):
    in_maps = stage_inputs(inputs)
    if "nc" not in _CACHE:
        _CACHE["nc"] = build(upto=8, dbg=())[0]
    nc = _CACHE["nc"]
    res = run_bass_kernel_spmd(nc, in_maps, core_ids=list(range(8)))
    out = np.empty((2, 4096, 2048), np.float32)
    for c in range(8):
        b, s = divmod(c, 4)
        y = np.asarray(res.results[c]["y"], np.float32)
        out[b, 1024 * s:1024 * (s + 1), :] = y.T
    return out
